# version 16
# baseline (speedup 1.0000x reference)
import numpy as np
import ml_dtypes

import concourse.bass as bass
import concourse.mybir as mybir
from concourse.bass_utils import run_bass_kernel_spmd
from contextlib import ExitStack

F32 = mybir.dt.float32
BF16 = mybir.dt.bfloat16
I32 = mybir.dt.int32
AF = mybir.ActivationFunctionType
ALU = mybir.AluOpType

P = 128
D = 2048
NDC = 16
DTR = 256
NST = 8
HID = 5632
NHC = 44
KXP = 272
LN_EPS = 1e-5

ENGINES = ("sync", "gpsimd", "vector", "scalar", "tensor")
_PEG = True


class Sem:
    def __init__(self, handle, name):
        self.h = handle
        self.name = name
        self.n = 0


def _flat(waits):
    out = []

    def rec(w):
        if w is None:
            return
        if isinstance(w, tuple) and len(w) == 2 and isinstance(w[0], Sem):
            out.append(w)
            return
        for x in w:
            rec(x)
    rec(waits)
    return out


class Ctx:
    def __init__(self, paranoid=False, eng_sems=None):
        self.streams = {e: [] for e in ENGINES}
        self.waited = {e: {} for e in ENGINES}
        self.paranoid = paranoid
        self.eng_sems = eng_sems or {}

    def emit(self, eng, fn, waits=(), inc=None, inc_k=1, is_dma=False):
        waits = list(_flat(waits))
        se = self.eng_sems.get(eng)
        if self.paranoid and se is not None:
            waits.append((se, se.n))
            if inc is None and not is_dma:
                inc = se
                inc_k = 1
        pruned = []
        for s, v in waits:
            if self.waited[eng].get(s.name, 0) < v:
                pruned.append((s, v))
                self.waited[eng][s.name] = v
        tok = None
        if inc is not None:
            inc.n += inc_k
            tok = (inc, inc.n)
        self.streams[eng].append((pruned, fn, inc, inc_k))
        return tok

    def replay(self, eng_name, handle):
        for pruned, fn, inc, inc_k in self.streams[eng_name]:
            for s, v in pruned:
                handle.wait_ge(s.h, v)
            inst = fn(handle)
            if inc is not None:
                inst.then_inc(inc.h, inc_k)


def build_nc(n_sc=4, paranoid=False, limit=99):
    TOK = 512 * n_sc
    NG = TOK // P
    nc = bass.Bass("TRN2", target_bir_lowering=False,
                   detect_race_conditions=paranoid)

    x_h = nc.dram_tensor("x", [TOK, D], F32, kind="ExternalInput").ap()
    wxt_h = nc.dram_tensor("wxt", [D, KXP], F32, kind="ExternalInput").ap()
    wdt_h = nc.dram_tensor("wdt", [DTR, D], F32, kind="ExternalInput").ap()
    alog_h = nc.dram_tensor("alog", [P, 128], F32, kind="ExternalInput").ap()
    bdt_h = nc.dram_tensor("bdt", [P, NDC], F32, kind="ExternalInput").ap()
    dpar_h = nc.dram_tensor("dpar", [P, NDC], F32, kind="ExternalInput").ap()
    wg_h = nc.dram_tensor("wg", [NHC, D, P], I32, kind="ExternalInput").ap()
    wu_h = nc.dram_tensor("wu", [NHC, D, P], I32, kind="ExternalInput").ap()
    wd_h = nc.dram_tensor("wd", [NHC, P, D], I32, kind="ExternalInput").ap()
    sg_h = nc.dram_tensor("sg", [P, NHC], F32, kind="ExternalInput").ap()
    su_h = nc.dram_tensor("su", [P, NHC], F32, kind="ExternalInput").ap()
    sd_h = nc.dram_tensor("sd", [P, NDC], F32, kind="ExternalInput").ap()
    sel_h = nc.dram_tensor("sel", [16, D], BF16, kind="ExternalInput").ap()
    idb_h = nc.dram_tensor("idb", [P, P], BF16, kind="ExternalInput").ap()
    out_h = nc.dram_tensor("out", [TOK, D], F32, kind="ExternalOutput").ap()

    wgbf = nc.dram_tensor("wgbf", [NHC, P, D], BF16, kind="Internal").ap()
    wubf = nc.dram_tensor("wubf", [NHC, P, D], BF16, kind="Internal").ap()
    wdbf = nc.dram_tensor("wdbf", [NHC, P, D], BF16, kind="Internal").ap()
    x2s = nc.dram_tensor("x2s", [NG, P, D], F32, kind="Internal").ap()
    prs = nc.dram_tensor("prs", [NHC, P, 512], BF16, kind="Internal").ap()
    bch = nc.dram_tensor("bch", [16, 512], BF16, kind="Internal").ap()

    x_rows = x_h.rearrange("(g p) d -> g p d", p=P)
    out_rows = out_h.rearrange("(g p) d -> g p d", p=P)
    wxt_r = wxt_h.rearrange("(c p) k -> p c k", p=P)
    wdt_r = wdt_h.rearrange("(c p) d -> p c d", p=P)

    es = ExitStack()
    SB = lambda name, shp, dt=F32: es.enter_context(nc.sbuf_tensor(name, shp, dt))
    with es:
        wxw = SB("wxw", [P, NDC * KXP], BF16)
        negA = SB("negA", [P, 128])
        bdt_c = SB("bdt_c", [P, NDC])
        dpar_c = SB("dpar_c", [P, NDC])
        sg_s = SB("sg_s", [P, NHC]); su_s = SB("su_s", [P, NHC])
        nsg_s = SB("nsg_s", [P, NHC]); sd_s = SB("sd_s", [P, NDC])
        idb_s = SB("idb_s", [P, P], BF16)
        c128_s = SB("c128_s", [1, P])
        onec_s = SB("onec_s", [P, 1], BF16)
        bc16 = SB("bc16", [16, 512], BF16)
        sumx3_s = SB("sumx3_s", [1, 512])
        sumpr_s = SB("sumpr_s", [1, 512])
        xb = [SB(f"xb{k}", [P, D]) for k in range(2)]
        xn1tm = SB("xn1tm", [P, D], BF16)
        xn1fm = SB("xn1fm", [P, NDC * 512], BF16)
        dtr_s = SB("dtr_s", [P, 1024], BF16)
        delta = [SB(f"delta{k}", [P, 512]) for k in range(2)]
        spA = SB("spA", [P, 512]); spB = SB("spB", [P, 512])
        u_t = [SB("u0", [P, 512])] * 2
        brep_all = SB("brep_all", [P, 8 * 512], BF16)
        crep_all = SB("crep_all", [P, 8 * 512], BF16)
        da_t = [SB(f"da{k}", [P, 512]) for k in range(2)]
        dbu_t = [SB(f"dbu{k}", [P, 512]) for k in range(2)]
        h_t = [SB(f"h{k}", [P, 512]) for k in range(2)]
        hc_t = [SB(f"hct{k}", [P, 512], BF16) for k in range(2)]
        yb = SB("yb0", [P, 512], BF16)
        yfm = [SB(f"yfm{k}", [P, 512], BF16) for k in range(2)]
        ytm = [SB(f"ytm{k}", [P, D], BF16) for k in range(4)]
        carry = SB("carry", [P, 128])
        st6 = SB("st6", [P, 24]); mv = SB("mv", [P, 2])
        nm_s = SB("nm_s", [P, 1]); rstd_s = SB("rstd_s", [P, 1]); lnt = SB("lnt", [P, 1])
        eps_s = SB("eps_s", [P, 1])
        spc_d = SB("spc_d", [P, 1]); spc_a = SB("spc_a", [P, 1])
        st6f = SB("st6f", [P, 24]); mvf = SB("mvf", [P, 2])
        nmf_s = SB("nmf_s", [P, 1]); rstdf_s = SB("rstdf_s", [P, 1]); lntf = SB("lntf", [P, 1])
        pool4 = [SB(f"pool4_{k}", [P, D]) for k in range(4)]
        xn3fm = SB("xn3fm", [P, NDC * 512], BF16)
        wgt = [SB(f"wgt{k}", [P, D], BF16) for k in range(2)]
        wut = [SB(f"wut{k}", [P, D], BF16) for k in range(2)]
        wdd = [SB(f"wdd{k}", [P, D], BF16) for k in range(2)]
        wddh = [wdd[0][:, 0:1024], wdd[0][:, 1024:2048],
                wdd[1][:, 0:1024], wdd[1][:, 1024:2048]]
        tclA = [SB("tclA0", [P, 512])]
        e1B = [SB("e1B0", [P, 512])]
        u1C = [SB("u1C0", [P, 512])]
        upt = [SB(f"upt{k}", [P, 512], BF16) for k in range(2)]
        gbt = [SB(f"gbt{k}", [P, 512], BF16) for k in range(2)]
        prt = [SB(f"prt{k}", [P, 512], BF16) for k in range(2)]
        prb = [SB(f"prb{k}", [P, 256], BF16) for k in range(2)]
        dfm = [SB(f"dfm{k}", [P, 256], BF16) for k in range(2)]
        dtm = [SB(f"dtm{k}", [P, D], BF16) for k in range(2)]

        ps = es.enter_context(nc.psum_tensor("ps", [P, 4096], F32))
        tbf = ps[:, 3072:3584].bitcast(BF16)

        sems = {}
        for nm in ("s_ini", "s_scr", "s_pr", "s_x2w", "s_rb",
                   "s_out", "s_cx", "s_pe", "s_dve", "s_act", "s_pl",
                   "s_xb0", "s_xb1", "s_cg0", "s_cg1", "s_cu0", "s_cu1",
                   "s_cd0", "s_cd1", "s_pb0", "s_pb1", "s_pb2",
                   "s_rd2", "s_rd3", "s_bch", "s_bc2",
                   "s_sg0", "s_sg1", "s_su0", "s_su1", "s_sd0", "s_sd1",
                   "s_pw0", "s_pw1",
                   "s_rg0", "s_rg1", "s_ru0", "s_ru1", "s_rd0", "s_rd1"):
            sems[nm] = Sem(es.enter_context(nc.semaphore(nm)), nm)
        s_ini, s_scr, s_pr, s_x2w, s_rb, s_out, s_cx, s_pe, s_dve, s_act, s_pl = (
            sems[k] for k in ("s_ini", "s_scr", "s_pr", "s_x2w", "s_rb",
                              "s_out", "s_cx", "s_pe", "s_dve", "s_act", "s_pl"))
        s_xb = [sems["s_xb0"], sems["s_xb1"]]
        s_cg = [sems["s_cg0"], sems["s_cg1"]]
        s_cu = [sems["s_cu0"], sems["s_cu1"]]
        s_cd = [sems["s_cd0"], sems["s_cd1"]]
        s_pb = [sems["s_pb0"], sems["s_pb1"], sems["s_pb2"]]
        s_sg = [sems["s_sg0"], sems["s_sg1"]]
        s_su2 = [sems["s_su0"], sems["s_su1"]]
        s_sd2 = [sems["s_sd0"], sems["s_sd1"]]
        s_pw = [sems["s_pw0"], sems["s_pw1"]]
        s_rg = [sems["s_rg0"], sems["s_rg1"]]
        s_ru = [sems["s_ru0"], sems["s_ru1"]]
        s_rd = [sems["s_rd0"], sems["s_rd1"], sems["s_rd2"], sems["s_rd3"]]
        s_bch = sems["s_bch"]; s_bc2 = sems["s_bc2"]

        ctx = Ctx(paranoid=paranoid,
                  eng_sems={"vector": s_dve, "scalar": s_act,
                            "tensor": s_pe, "gpsimd": s_pl})
        E = ctx.emit

        def gate_ps(hc):
            o = (hc % 2) * 512
            return ps[:, o:o + 512]

        def up_ps(hc):
            o = 1024 + (hc % 2) * 512
            return ps[:, o:o + 512]

        def mm_ps(j):
            o = 2048 + (j % 2) * 512
            return ps[:, o:o + 512]

        tbf4 = ps[:, 2048:2560].bitcast(BF16)
        tbf5 = ps[:, 2560:3072].bitcast(BF16)
        tr_views = {4: tbf4, 5: tbf5}
        tr_st = {"k": 0, "ev": {4: None, 5: None, 6: None}}

        def emit_tr(src_ap, dst_ap, waits, banks=(6,)):
            k = tr_st["k"]; tr_st["k"] += 1
            bank = banks[k % len(banks)]
            pos = (k // len(banks)) % 4
            view = tbf if bank == 6 else tr_views[bank]
            slot = view[:, pos * 128:(pos + 1) * 128]
            t_tr = E("tensor",
                     (lambda sl_, sa_: lambda e: e.transpose(sl_, sa_, idb_s[:]))(slot, src_ap),
                     waits=list(_flat(waits)) + list(_flat(tr_st["ev"][bank])), inc=s_pe)
            t_ev = E("vector",
                     (lambda da_, sl_: lambda e: e.tensor_copy(da_, sl_))(dst_ap, slot),
                     waits=[t_tr], inc=s_dve)
            tr_st["ev"][bank] = t_ev
            return t_ev

        def tr_rotate_begin():
            tr_st["k"] = 0
            tr_st["ev"][4] = [tr_st["ev"][4], mmslot_free.get(0)]
            tr_st["ev"][5] = [tr_st["ev"][5], mmslot_free.get(1)]

        def tr_rotate_end():
            mmslot_free[0] = [mmslot_free.get(0), tr_st["ev"][4]]
            mmslot_free[1] = [mmslot_free.get(1), tr_st["ev"][5]]

        def dn_ps(dc):
            return ps[:, dc * 256:dc * 256 + 256]

        sums_ps = ps[0:1, 3584:4096]

        free = {}
        tbf_free = {}
        mmslot_free = {}

        def dma(dst, src):
            return lambda e: e.dma_start(out=dst, in_=src)

        E("sync", dma(negA[:], alog_h), inc=s_ini, inc_k=16)
        E("sync", dma(bdt_c[:], bdt_h), inc=s_ini, inc_k=16)
        E("sync", dma(dpar_c[:], dpar_h), inc=s_ini, inc_k=16)
        E("sync", dma(sg_s[:], sg_h), inc=s_ini, inc_k=16)
        E("sync", dma(su_s[:], su_h), inc=s_ini, inc_k=16)
        E("sync", dma(sd_s[:], sd_h), inc=s_ini, inc_k=16)
        E("sync", dma(idb_s[:], idb_h), inc=s_ini, inc_k=16)
        t_init = (s_ini, s_ini.n)
        t_eA = E("scalar", lambda e: e.activation(negA[:], negA[:], AF.Exp),
                 waits=[t_init], inc=s_act)
        t_negA = E("vector", lambda e: e.tensor_scalar_mul(negA[:], negA[:], -1.0),
                   waits=[t_eA], inc=s_dve)
        t_nsg = E("vector", lambda e: e.tensor_scalar_mul(nsg_s[:], sg_s[:], -1.0),
                  waits=[t_init], inc=s_dve)
        E("vector", lambda e: e.memset(carry[:], 0.0), inc=s_dve)
        t_eps = E("vector", lambda e: e.memset(eps_s[:], LN_EPS), inc=s_dve)
        t_c128 = E("vector", lambda e: e.memset(c128_s[:], -128.0), inc=s_dve)
        t_onec = E("vector", lambda e: e.memset(onec_s[:], 1.0), inc=s_dve)

        def emit_ln(src_ap_fn, n_groups, extra_first_waits, fset=False):
            ST, MV, NM, RS, LT = ((st6f, mvf, nmf_s, rstdf_s, lntf) if fset
                                  else (st6, mv, nm_s, rstd_s, lnt))
            mk = "mvf" if fset else "mv"
            rk = "rstdf" if fset else "rstd"
            w_mv = free.get(mk)
            for g in range(n_groups):
                E("vector",
                  (lambda g_: lambda e: e.bn_stats(ST[:, g_ * 6:g_ * 6 + 6], src_ap_fn(g_)))(g),
                  waits=(extra_first_waits if g == 0 else []), inc=s_dve)
            E("vector", lambda e: e.tensor_copy(spc_d[:], ST[:, 0:1]), inc=s_dve)
            E("vector", lambda e: e.bn_aggr(MV[:], ST[:, 0:6 * n_groups]),
              waits=[w_mv], inc=s_dve)
            t_ag = E("vector", lambda e: e.tensor_copy(spc_d[:], MV[:, 0:1]),
                     inc=s_dve)
            t_nm = E("vector", lambda e: e.tensor_scalar_mul(NM[:], MV[:, 0:1], -1.0),
                     waits=[t_ag], inc=s_dve)
            t_ln = E("scalar", lambda e: e.activation(LT[:], MV[:, 1:2], AF.Ln, bias=eps_s[:, 0:1]),
                     waits=[t_ag, t_eps], inc=s_act)
            E("scalar", lambda e: e.activation(RS[:], LT[:], AF.Exp, scale=-0.5),
              waits=[t_ln, free.get(rk)], inc=s_act)
            t_rs = E("scalar", lambda e: e.copy(spc_a[:], RS[:]),
                     inc=s_act)
            free[mk] = t_ln
            return t_nm, t_rs

        tok_xn3_free = None
        tok_x2w = [None] * NG
        xn3_tok = {}
        x2w_tok = {}
        sh = {"xn3_epoch": -1, "mamba_done": -1}
        scr_tok_g = [None] * NHC
        scr_tok_u = [None] * NHC
        scr_tok_d = [None] * NHC
        dn_ev = [None] * 8

        def gen_wcast():
            for hc in range(NHC):
                wsl = hc % 2
                t_lg = E("gpsimd", (lambda h_, s_: lambda e: e.dma_start(
                    out=wgt[s_].rearrange("p (c h) -> p c h", c=NDC),
                    in_=wg_h[h_].rearrange("(c p) h -> p c h", p=P)))(hc, wsl),
                    waits=[free.get(("wgt", wsl))], inc=s_cg[wsl], inc_k=16)
                t_sgw = E("sync", (lambda h_, s_: lambda e: e.dma_start(
                    out=wgbf[h_], in_=wgt[s_][:]))(hc, wsl), waits=[t_lg],
                    inc=s_sg[wsl], inc_k=16)
                free[("wgt", wsl)] = t_sgw
                scr_tok_g[hc] = t_sgw
                yield 2600
                t_lu = E("gpsimd", (lambda h_, s_: lambda e: e.dma_start(
                    out=wut[s_].rearrange("p (c h) -> p c h", c=NDC),
                    in_=wu_h[h_].rearrange("(c p) h -> p c h", p=P)))(hc, wsl),
                    waits=[free.get(("wut", wsl))], inc=s_cu[wsl], inc_k=16)
                t_suw = E("sync", (lambda h_, s_: lambda e: e.dma_start(
                    out=wubf[h_], in_=wut[s_][:]))(hc, wsl), waits=[t_lu],
                    inc=s_su2[wsl], inc_k=16)
                free[("wut", wsl)] = t_suw
                scr_tok_u[hc] = t_suw
                yield 2600
                t_ldc = E("gpsimd", (lambda h_, s_: lambda e: e.dma_start(
                    out=wdd[s_][:], in_=wd_h[h_]))(hc, wsl),
                    waits=[free.get(("wdd", wsl))], inc=s_cd[wsl], inc_k=16)
                t_sdw = E("sync", (lambda h_, s_: lambda e: e.dma_start(
                    out=wdbf[h_], in_=wdd[s_][:]))(hc, wsl), waits=[t_ldc],
                    inc=s_sd2[wsl], inc_k=16)
                free[("wdd", wsl)] = t_sdw
                scr_tok_d[hc] = t_sdw
                yield 2600
            for q in range(4):
                free[("wddh", q)] = free.get(("wdd", q // 2))

        def gen_mamba(isc):
            nonlocal tok_xn3_free
            tok_xn1tm_free = free.get("xn1tm")
            tr_rotate_begin()
            for tc in range(4):
                g = 4 * isc + tc
                sl = g % 2
                t_xin = E("sync", (lambda g_, s_: lambda e: e.dma_start(
                    out=xb[s_][:], in_=x_rows[g_]))(g, sl),
                    waits=[free.get(("xb", sl))], inc=s_xb[sl], inc_k=16)
                t_nm, t_rs = emit_ln(
                    (lambda s_: lambda g_: xb[s_][:, g_ * 512:g_ * 512 + 512])(sl),
                    4, [t_xin])
                t_xn1 = E("vector", (lambda s_: lambda e: e.tensor_scalar(
                    xn1tm[:], xb[s_][:], nm_s[:, 0:1], rstd_s[:, 0:1], ALU.add, ALU.mult))(sl),
                    waits=[t_nm, t_rs, tok_xn1tm_free], inc=s_dve)
                t_xn1 = E("vector", lambda e: e.tensor_copy(lnt[:], xn1tm[:, 0:1]),
                          inc=s_dve)
                free["rstd"] = t_xn1
                free[("xb", sl)] = t_xn1
                yield 4500
                last_ev = None
                for dc in range(NDC):
                    wts = [t_xn1, t_init]
                    last_ev = emit_tr(
                        xn1tm[:, dc * 128:dc * 128 + 128],
                        xn1fm[:, dc * 512 + tc * 128:dc * 512 + tc * 128 + 128],
                        wts, banks=(6, 4, 5))
                    if dc % 2 == 1:
                        yield 650
                tok_xn1tm_free = last_ev
            tr_rotate_end()
            free["xn1tm"] = tok_xn1tm_free
            t_xn1fm_done = tok_xn1tm_free

            if limit < 2:
                return
            t_wx = E("gpsimd", lambda e: e.dma_start(
                out=wxw.rearrange("p (c k) -> p c k", c=NDC), in_=wxt_r),
                waits=[free.get("wxw")], inc=s_cx, inc_k=16)
            evs = []
            for m in range(3):
                mw = 128 if m < 2 else 16
                c0 = m * 128
                j = m % 2
                wts = [t_wx, t_xn1fm_done, mmslot_free.get(j)]
                if m == 2:
                    wts.append(evs[0])
                tprev = None
                for dc in range(NDC):
                    tprev = E("tensor", (lambda j_, dc_, c0_, mw_: lambda e: e.matmul(
                        mm_ps(j_)[0:mw_, :],
                        wxw[:, dc_ * KXP + c0_:dc_ * KXP + c0_ + mw_],
                        xn1fm[:, dc_ * 512:dc_ * 512 + 512],
                        start=(dc_ == 0), stop=(dc_ == NDC - 1)))(j, dc, c0, mw),
                        waits=(wts if dc == 0 else []),
                        inc=(s_pe if dc == NDC - 1 else None))
                if m < 2:
                    tev = E("scalar", (lambda j_, m_: lambda e: e.copy(
                        dtr_s[:, m_ * 512:m_ * 512 + 512], mm_ps(j_)[:]))(j, m),
                        waits=[tprev], inc=s_act)
                else:
                    tev = E("scalar", (lambda j_: lambda e: e.copy(
                        bc16[:], mm_ps(j_)[0:16, :]))(j),
                        waits=[tprev], inc=s_act)
                mmslot_free[j] = tev
                evs.append(tev)
                yield 2000
            t_dtr01 = evs[1]
            t_bc = evs[2]
            t_m2mm = (s_pe, s_pe.n)
            t_bchw = E("sync", lambda e: e.dma_start(out=bch, in_=bc16[:]),
                       waits=[t_bc, free.get("bch")], inc=s_bch, inc_k=16)
            for n16 in range(16):
                dst = brep_all if n16 < 8 else crep_all
                E("sync", (lambda d_, n_: lambda e: e.dma_start(
                    out=d_[:, (n_ % 8) * 512:(n_ % 8) * 512 + 512],
                    in_=bch[n_:n_ + 1, :].broadcast_to([P, 512])))(dst, n16),
                    waits=([t_bchw, free.get("bcast_bufs")] if n16 == 0 else []),
                    inc=s_bc2, inc_k=16)
            t_bcast = (s_bc2, s_bc2.n)
            free["bch"] = t_bcast
            yield 2000
            t_wd = E("gpsimd", lambda e: e.dma_start(
                out=wxw[:, 0:2 * D].rearrange("p (c d) -> p c d", c=2), in_=wdt_r),
                waits=[t_m2mm], inc=s_cx, inc_k=16)
            wdt_view = wxw[:, 0:2 * D]

            if limit < 3:
                return
            t_last_ytr = None
            t_db = t_hc = None
            for dc in range(NDC):
                dsl = dc % 2
                E("tensor", (lambda dc_: lambda e: e.matmul(
                    mm_ps(0), wdt_view[:, dc_ * 128:dc_ * 128 + 128],
                    dtr_s[:, 0:512], start=True, stop=False))(dc),
                    waits=[t_wd, evs[0], t_dtr01, mmslot_free.get(0)])
                t_dmm = E("tensor", (lambda dc_: lambda e: e.matmul(
                    mm_ps(0), wdt_view[:, D + dc_ * 128:D + dc_ * 128 + 128],
                    dtr_s[:, 512:1024], start=False, stop=True))(dc), inc=s_pe)
                t1 = E("scalar", (lambda dc_: lambda e: e.activation(
                    spA[:], mm_ps(0), AF.Relu, bias=bdt_c[:, dc_:dc_ + 1]))(dc),
                    waits=[t_dmm, free.get("spA")], inc=s_act)
                t2 = E("scalar", (lambda dc_: lambda e: e.activation(
                    spB[:], mm_ps(0), AF.Abs, bias=bdt_c[:, dc_:dc_ + 1]))(dc),
                    waits=[free.get("spB")], inc=s_act)
                mmslot_free[0] = t2
                t3 = E("scalar", (lambda d_: lambda e: e.activation(
                    delta[d_][:], spB[:], AF.Exp, scale=-1.0))(dsl),
                    waits=[free.get(("delta", dsl))], inc=s_act)
                t4 = E("scalar", (lambda d_: lambda e: e.activation(
                    spB[:], delta[d_][:], AF.Ln, bias=1.0))(dsl), inc=s_act)
                t_delta = E("vector", (lambda d_: lambda e: e.tensor_add(
                    delta[d_][:], spA[:], spB[:]))(dsl),
                    waits=[t4], inc=s_dve)
                free["spA"] = t_delta
                free["spB"] = t_delta
                t_u = E("gpsimd", (lambda d_, dc_: lambda e: e.tensor_mul(
                    u_t[d_][:], delta[d_][:], xn1fm[:, dc_ * 512:dc_ * 512 + 512]))(dsl, dc),
                    waits=[t_delta], inc=s_pl)
                t_ts = E("vector", (lambda dc_: lambda e: e.tensor_scalar(
                    yb[:], xn1fm[:, dc_ * 512:dc_ * 512 + 512],
                    dpar_c[:, dc_:dc_ + 1], None, ALU.mult))(dc),
                    waits=[t_xn1fm_done, free.get("yb")], inc=s_dve)
                t_ts = E("vector", lambda e: e.tensor_copy(spc_d[:], yb[:, 0:1]),
                         inc=s_dve)
                pend_ym = ("start", [t_ts, mmslot_free.get(1), t_init])
                yield 3500
                last_da = None
                t_ym = None
                pend_carry = None

                def flush_ym():
                    nonlocal pend_ym, t_ym
                    if pend_ym is None:
                        return
                    kind, wts_ = pend_ym
                    if kind == "start":
                        t_ym = E("tensor", lambda e: e.matmul(
                            mm_ps(1), idb_s[:], yb[:], start=True, stop=False),
                            waits=wts_, inc=s_pe)
                        free["yb"] = t_ym
                    else:
                        n_, s_ = kind
                        t_ym = E("tensor", (lambda ss, nn: lambda e: e.matmul(
                            mm_ps(1), idb_s[:], hc_t[ss][:],
                            start=False, stop=(nn == NST - 1)))(s_, n_),
                            waits=wts_, inc=s_pe)
                        free[("hc", s_)] = t_ym
                    pend_ym = None

                def emit_dbu(n_):
                    return E("gpsimd", (lambda d_, s_, nn: lambda e: e.tensor_mul(
                        dbu_t[s_][:], u_t[d_][:],
                        brep_all[:, nn * 512:nn * 512 + 512]))(dsl, n_ % 2, n_),
                        waits=[t_u, t_bcast, free.get(("dbu", n_ % 2))], inc=s_pl)

                def emit_da(n_):
                    return E("scalar", (lambda d_, s_, dc_, nn: lambda e: e.activation(
                        da_t[s_][:], delta[d_][:], AF.Exp,
                        scale=negA[:, dc_ * 8 + nn:dc_ * 8 + nn + 1]))(dsl, n_ % 2, dc, n_),
                        waits=[t_negA, free.get(("da", n_ % 2))]
                              + ([t_delta] if n_ == 0 else []), inc=s_act)

                t_db_pre = emit_dbu(0)
                t_da_pre = emit_da(0)
                for n in range(NST):
                    nsl = n % 2
                    flush_ym()
                    t_da = t_da_pre
                    last_da = t_da
                    if n + 1 < NST:
                        t_da_pre = emit_da(n + 1)
                    t_db = t_db_pre
                    if n + 1 < NST:
                        t_db_pre = emit_dbu(n + 1)
                    idx = dc * 8 + n
                    t_sc = E("vector", (lambda s_, i_: lambda e: e.tensor_tensor_scan(
                        h_t[s_][:], da_t[s_][:], dbu_t[s_][:], carry[:, i_:i_ + 1],
                        ALU.mult, ALU.add))(nsl, idx),
                        waits=[t_da, t_db, free.get(("h", nsl))], inc=s_dve)
                    free[("da", nsl)] = t_sc
                    free[("dbu", nsl)] = t_sc
                    if pend_carry is not None:
                        ps_, pi_ = pend_carry
                        tcc = E("vector", (lambda s_, i_: lambda e: e.tensor_copy(
                            carry[:, i_:i_ + 1], h_t[s_][:, 511:512]))(ps_, pi_),
                            inc=s_dve)
                        free[("h", ps_)] = tcc
                    pend_carry = (nsl, idx)
                    use_pool = (n % 2 == 0)
                    t_hc = E("gpsimd" if use_pool else "vector",
                             (lambda s_, n_: lambda e: e.tensor_mul(
                                 hc_t[s_][:], h_t[s_][:],
                                 crep_all[:, n_ * 512:n_ * 512 + 512]))(nsl, n),
                             waits=[t_sc, t_bcast, free.get(("hc", nsl))],
                             inc=(s_pl if use_pool else s_dve))
                    if not use_pool:
                        t_hc = E("vector", (lambda s_: lambda e: e.tensor_copy(
                            spc_d[:], hc_t[s_][:, 0:1]))(nsl), inc=s_dve)
                    pend_ym = ((n, nsl), [t_hc])
                    yield 2000
                flush_ym()
                ps_, pi_ = pend_carry
                tcc = E("vector", (lambda s_, i_: lambda e: e.tensor_copy(
                    carry[:, i_:i_ + 1], h_t[s_][:, 511:512]))(ps_, pi_),
                    inc=s_dve)
                free[("h", ps_)] = tcc
                free[("delta", dsl)] = [last_da, t_u]
                t_yf = E("scalar", (lambda f_: lambda e: e.activation(
                    yfm[f_][:], mm_ps(1), AF.Copy))(dsl),
                    waits=[t_ym, free.get(("yfm", dsl))], inc=s_act)
                t_yfs = E("scalar", (lambda f_: lambda e: e.copy(
                    spc_a[:], yfm[f_][:, 0:1]))(dsl), inc=s_act)
                mmslot_free[1] = t_yf
                for jj in range(4):
                    t_ev = emit_tr(
                        yfm[dsl][:, jj * 128:jj * 128 + 128],
                        ytm[jj][:, dc * 128:dc * 128 + 128],
                        [t_yf, t_yfs])
                    if jj == 1:
                        yield 650
                    if jj == 3:
                        free[("yfm", dsl)] = t_ev
                        t_last_ytr = t_ev
                yield 650
            free["bcast_bufs"] = [t_db, t_hc]
            free["wxw"] = t_dmm

            if limit < 4:
                return
            while sh["xn3_epoch"] < isc - 1:
                yield ("blk", isc - 1)
            tok_xn1tm_free = free.get("xn1tm")
            tr_rotate_begin()
            for tc in range(4):
                g = 4 * isc + tc
                sl = g % 2
                t_xin = E("sync", (lambda g_, s_: lambda e: e.dma_start(
                    out=xb[s_][:], in_=x_rows[g_]))(g, sl),
                    waits=[free.get(("xb", sl))], inc=s_xb[sl], inc_k=16)
                t_nm, t_rs = emit_ln(
                    (lambda t_: lambda g_: ytm[t_][:, g_ * 512:g_ * 512 + 512])(tc),
                    4, [t_last_ytr])
                t_t2 = E("vector", (lambda t_: lambda e: e.tensor_scalar(
                    pool4[0][:], ytm[t_][:], nm_s[:, 0:1], rstd_s[:, 0:1],
                    ALU.add, ALU.mult))(tc),
                    waits=[t_nm, t_rs, free.get(("pool4", 0))], inc=s_dve)
                free["rstd"] = t_t2
                t_x2 = E("vector", (lambda s_: lambda e: e.tensor_add(
                    pool4[1][:], xb[s_][:], pool4[0][:]))(sl),
                    waits=[t_xin, t_t2, free.get(("pool4", 1))], inc=s_dve)
                free[("pool4", 0)] = t_x2
                free[("xb", sl)] = t_x2
                t_w2 = E("sync", (lambda g_: lambda e: e.dma_start(
                    out=x2s[g_], in_=pool4[1][:]))(g), waits=[t_x2], inc=s_x2w, inc_k=16)
                tok_x2w[g] = t_w2
                yield 5000
                t_nm, t_rs = emit_ln(
                    lambda g_: pool4[1][:, g_ * 512:g_ * 512 + 512], 4, [t_x2])
                t_xn3 = E("vector", lambda e: e.tensor_scalar(
                    xn1tm[:], pool4[1][:], nm_s[:, 0:1], rstd_s[:, 0:1], ALU.add, ALU.mult),
                    waits=[t_nm, t_rs, tok_xn1tm_free], inc=s_dve)
                t_xn3 = E("vector", lambda e: e.tensor_copy(lnt[:], xn1tm[:, 0:1]),
                          inc=s_dve)
                free["rstd"] = t_xn3
                free[("pool4", 1)] = [t_xn3, t_w2]
                yield 4500
                last_ev = None
                for dc in range(NDC):
                    last_ev = emit_tr(
                        xn1tm[:, dc * 128:dc * 128 + 128],
                        xn3fm[:, dc * 512 + tc * 128:dc * 512 + tc * 128 + 128],
                        [t_xn3, tok_xn3_free], banks=(6, 4, 5))
                    tok_xn3_free = None
                    if dc % 2 == 1:
                        yield 650
                tok_xn1tm_free = last_ev
            tr_rotate_end()
            free["xn1tm"] = tok_xn1tm_free
            xn3_tok[isc] = tok_xn1tm_free
            x2w_tok[isc] = (s_x2w, s_x2w.n)
            sh["mamba_done"] = isc

        def gen_ffn(isc):
            nonlocal tok_xn3_free
            if limit < 5:
                return
            while sh["mamba_done"] < isc:
                yield ("blk_m", isc)
            tprev = None
            for dc in range(NDC):
                tprev = E("tensor", (lambda dc_: lambda e: e.matmul(
                    sums_ps[:, 0:512], onec_s[:], xn3fm[:, dc_ * 512:dc_ * 512 + 512],
                    start=(dc_ == 0), stop=(dc_ == NDC - 1)))(dc),
                    waits=([xn3_tok[isc], t_onec, free.get("sums")] if dc == 0 else []),
                    inc=(s_pe if dc == NDC - 1 else None))
            E("scalar", lambda e: e.copy(sumx3_s[:], sums_ps[:, 0:512]),
              waits=[tprev, free.get("sumx3")], inc=s_act)
            t_sx3 = E("scalar", lambda e: e.copy(spc_a[0:1, 0:1], sumx3_s[0:1, 0:1]),
                      inc=s_act)
            free["sums"] = t_sx3
            yield 1500

            t_pr_wr = [None] * NHC
            t_last_gu = None
            t_spr_last = None
            pend_spr = None
            pend_wpr = None
            for hc in range(NHC if limit >= 42 else 0):
                wsl = hc % 2
                t_lg = E("sync", (lambda h_, s_: lambda e: e.dma_start(
                    out=wgt[s_][:], in_=wgbf[h_]))(hc, wsl),
                    waits=[free.get(("wgt", wsl)), scr_tok_g[hc]],
                    inc=s_rg[wsl], inc_k=16)
                t_lu = E("sync", (lambda h_, s_: lambda e: e.dma_start(
                    out=wut[s_][:], in_=wubf[h_]))(hc, wsl),
                    waits=[free.get(("wut", wsl)), scr_tok_u[hc]],
                    inc=s_ru[wsl], inc_k=16)
                t_sgw = t_suw = None
                if pend_wpr is not None:
                    h_p, s_p, w_p = pend_wpr
                    t_wpr = E("sync", (lambda h_, s_: lambda e: e.dma_start(
                        out=prs[h_], in_=prt[s_][:]))(h_p, s_p),
                        waits=w_p, inc=s_pw[s_p], inc_k=16)
                    free[("prt", s_p)] = [free.get(("prt", s_p)), t_wpr]
                    pend_wpr = None
                if limit < 43:
                    free[("wgt", wsl)] = t_sgw if t_sgw is not None else t_lg
                    free[("wut", wsl)] = t_suw if t_suw is not None else t_lu
                    continue
                use_corr = limit != 441
                if limit >= 460 and limit <= 469 and hc >= (limit - 460):
                    free[("wgt", wsl)] = t_sgw if t_sgw is not None else t_lg
                    free[("wut", wsl)] = t_suw if t_suw is not None else t_lu
                    continue
                lhs_fn = (lambda s_, dc_: wxw[:, dc_ * KXP:dc_ * KXP + 128]) if limit == 445 \
                    else (lambda s_, dc_: wgt[s_][:, dc_ * 128:dc_ * 128 + 128])
                rhs_fn = (lambda dc_: xn1fm[:, dc_ * 512:dc_ * 512 + 512]) if limit == 444 \
                    else (lambda dc_: xn3fm[:, dc_ * 512:dc_ * 512 + 512])
                for dc in range(NDC):
                    E("tensor", (lambda h_, s_, dc_: lambda e: e.matmul(
                        gate_ps(h_), lhs_fn(s_, dc_), rhs_fn(dc_),
                        start=(dc_ == 0),
                        stop=(not use_corr and dc_ == NDC - 1)))(hc, wsl, dc),
                        waits=([t_lg, t_sx3, t_c128, free.get(("gps", hc % 2))]
                               if dc == 0 else []),
                        inc=(s_pe if (not use_corr and dc == NDC - 1) else None))
                    if dc == 7:
                        yield 2400
                if use_corr:
                    t_gmm = E("tensor", (lambda h_: lambda e: e.matmul(
                        gate_ps(h_), c128_s[:], sumx3_s[:], start=False, stop=True))(hc),
                        inc=s_pe)
                else:
                    t_gmm = (s_pe, s_pe.n)
                yield 2800
                if pend_spr is not None:
                    h_p, s_p, w_p = pend_spr
                    t_spr = E("tensor", (lambda h_, s_: lambda e: e.matmul(
                        sums_ps[:, 0:512], onec_s[:], prt[s_][:],
                        start=(h_ == 0), stop=(h_ == NHC - 1)))(h_p, s_p),
                        waits=w_p, inc=s_pe)
                    t_spr_last = t_spr
                    free[("prt", s_p)] = [free.get(("prt", s_p)), t_spr]
                    pend_spr = None
                for dc in range(NDC if (limit >= 44 and not (440 <= limit <= 469)) else 0):
                    E("tensor", (lambda h_, s_, dc_: lambda e: e.matmul(
                        up_ps(h_), wut[s_][:, dc_ * 128:dc_ * 128 + 128],
                        xn3fm[:, dc_ * 512:dc_ * 512 + 512],
                        start=(dc_ == 0), stop=False))(hc, wsl, dc),
                        waits=([t_lu, free.get(("ups", hc % 2))] if dc == 0 else []))
                    if dc == 7:
                        yield 2400
                if limit >= 44 and not (440 <= limit <= 469):
                    t_umm = E("tensor", (lambda h_: lambda e: e.matmul(
                        up_ps(h_), c128_s[:], sumx3_s[:], start=False, stop=True))(hc),
                        inc=s_pe)
                else:
                    t_umm = t_gmm
                t_last_gu = t_umm
                free[("wgt", wsl)] = [t_gmm, t_sgw]
                free[("wut", wsl)] = [t_umm, t_suw]
                esl = hc % 2
                t_u1 = E("scalar", (lambda h_: lambda e: e.activation(
                    u1C[0][:], gate_ps(h_), AF.Copy, scale=sg_s[:, h_:h_ + 1]))(hc),
                    waits=[t_gmm, free.get("u1C")], inc=s_act)
                free[("gps", hc % 2)] = t_u1
                if limit < 45 or (440 <= limit <= 469):
                    free[("ups", hc % 2)] = t_umm
                    continue
                t_e1 = E("scalar", lambda e: e.activation(
                    e1B[0][:], u1C[0][:], AF.Exp, scale=-1.0),
                    waits=[t_u1, free.get("e1B")], inc=s_act)
                t_l1 = E("scalar", lambda e: e.activation(
                    tclA[0][:], e1B[0][:], AF.Ln, bias=1.0), inc=s_act)
                t_s1 = E("scalar", lambda e: e.activation(
                    e1B[0][:], tclA[0][:], AF.Exp, scale=-1.0), inc=s_act)
                free["tclA"] = t_s1
                t_gb = E("gpsimd", (lambda s_: lambda e: e.tensor_mul(
                    gbt[s_][:], u1C[0][:], e1B[0][:]))(esl),
                    waits=[t_s1, t_u1, free.get(("gbt", esl))], inc=s_pl)
                free["e1B"] = t_gb
                free["u1C"] = t_gb
                t_up = E("vector", (lambda h_, s_: lambda e: e.tensor_scalar(
                    upt[s_][:], up_ps(h_), su_s[:, h_:h_ + 1], None, ALU.mult))(hc, esl),
                    waits=[t_umm, free.get(("upt", esl))], inc=s_dve)
                free[("ups", hc % 2)] = t_up
                if limit < 46 or (440 <= limit <= 469):
                    continue
                t_pr = E("vector", (lambda s_: lambda e: e.tensor_mul(
                    prt[s_][:], gbt[s_][:], upt[s_][:]))(esl),
                    waits=[t_gb, t_up, free.get(("prt", esl))], inc=s_dve)
                free[("gbt", esl)] = t_pr
                free[("upt", esl)] = t_pr
                pend_spr = (hc, esl, [t_pr] + ([free.get("sums")] if hc == 0 else []))
                pend_wpr = (hc, esl,
                            [t_pr] + ([free.get("prs_hbm")] if hc == 0 else []))
                free[("prt", esl)] = t_pr
                yield 2800
            if pend_wpr is not None:
                h_p, s_p, w_p = pend_wpr
                t_wpr = E("sync", (lambda h_, s_: lambda e: e.dma_start(
                    out=prs[h_], in_=prt[s_][:]))(h_p, s_p),
                    waits=w_p, inc=s_pw[s_p], inc_k=16)
                free[("prt", s_p)] = [free.get(("prt", s_p)), t_wpr]
                pend_wpr = None
            if pend_spr is not None:
                h_p, s_p, w_p = pend_spr
                t_spr = E("tensor", (lambda h_, s_: lambda e: e.matmul(
                    sums_ps[:, 0:512], onec_s[:], prt[s_][:],
                    start=(h_ == 0), stop=(h_ == NHC - 1)))(h_p, s_p),
                    waits=w_p, inc=s_pe)
                t_spr_last = t_spr
                free[("prt", s_p)] = [free.get(("prt", s_p)), t_spr]
                pend_spr = None
            tok_xn3_free = t_last_gu
            sh["xn3_epoch"] = isc
            if limit < 47 or (440 <= limit <= 469):
                return
            t_pr_all = [(s_pw[0], s_pw[0].n), (s_pw[1], s_pw[1].n)]
            E("scalar", lambda e: e.copy(sumpr_s[:], sums_ps[:, 0:512]),
              waits=[t_spr_last, free.get("sumpr")], inc=s_act)
            t_sumpr = E("scalar", lambda e: e.copy(spc_a[0:1, 0:1], sumpr_s[0:1, 0:1]),
                        inc=s_act)
            free["sums"] = t_sumpr
            free["sumx3"] = t_gmm
            yield 1000

            if limit < 48 or (440 <= limit <= 469):
                return
            t_out_sc = []
            t_corr_last_all = None
            for th in range(2):
                t_pieces = []
                for dh in range(2):
                    pidx = th * 2 + dh
                    for hc in range(NHC):
                        wsl = hc % 4
                        t_ld = E("sync", (lambda h_, s_, dh_: lambda e: e.dma_start(
                            out=wddh[s_],
                            in_=wdbf[h_][:, dh_ * 1024:dh_ * 1024 + 1024]))(hc, wsl, dh),
                            waits=[free.get(("wddh", wsl)), scr_tok_d[hc]],
                            inc=s_rd[wsl], inc_k=16)
                        psl = hc % 2
                        t_lp = E("sync", (lambda h_, s_, th_: lambda e: e.dma_start(
                            out=prb[s_][:], in_=prs[h_][:, th_ * 256:th_ * 256 + 256]))(hc, psl, th),
                            waits=[free.get(("prb", psl)), t_pr_all], inc=s_pb[psl], inc_k=16)
                        tlast = None
                        for dc8 in range(8):
                            wts = []
                            if hc == 0 and dc8 % 2 == 0:
                                wts += [dn_ev[dc8], dn_ev[dc8 + 1]]
                                if pidx == 0:
                                    wts.append((free.get(("gps", 0)), free.get(("gps", 1)),
                                                free.get(("ups", 0)), free.get(("ups", 1))
                                                )[dc8 // 2])
                            if dc8 == 0:
                                wts += [t_ld, t_lp]
                            tlast = E("tensor", (lambda h_, s_, p_, d8_: lambda e: e.matmul(
                                ps[:, d8_ * 256:d8_ * 256 + 256],
                                wddh[s_][:, d8_ * 128:d8_ * 128 + 128], prb[p_][:],
                                start=(h_ == 0 and d8_ % 2 == 0), stop=False,
                                skip_group_check=True))(hc, wsl, psl, dc8),
                                waits=wts, inc=(s_pe if dc8 == 7 else None))
                        free[("wddh", wsl)] = tlast
                        free[("prb", psl)] = tlast
                        yield 1400
                    t_corr = None
                    for dc8 in range(8):
                        t_corr = E("tensor", (lambda d8_, th_: lambda e: e.matmul(
                            ps[:, d8_ * 256:d8_ * 256 + 256], c128_s[:],
                            sumpr_s[0:1, th_ * 256:th_ * 256 + 256],
                            start=False, stop=(d8_ % 2 == 1),
                            skip_group_check=True))(dc8, th),
                            waits=([t_sumpr] if dc8 == 0 else []), inc=s_pe)
                    t_corr_last_all = t_corr
                    yield 400
                    for dc8 in range(8):
                        dc = dh * 8 + dc8
                        esl = dc % 2
                        E("scalar", (lambda dc_, d8_, s_: lambda e: e.activation(
                            dfm[s_][:], ps[:, d8_ * 256:d8_ * 256 + 256],
                            AF.Copy, scale=sd_s[:, dc_:dc_ + 1]))(dc, dc8, esl),
                            waits=[t_corr, free.get(("dfm", esl))], inc=s_act)
                        t_df = E("scalar", (lambda s_: lambda e: e.copy(
                            spc_a[:], dfm[s_][:, 0:1]))(esl), inc=s_act)
                        dn_ev[dc8] = t_df
                        for jj in range(2):
                            t_ev = emit_tr(
                                dfm[esl][:, jj * 128:jj * 128 + 128],
                                dtm[jj][:, dc * 128:dc * 128 + 128],
                                [t_df])
                            t_pieces.append(t_ev)
                        free[("dfm", esl)] = t_pieces[-1]
                        yield 1000
                t_dtm_done = t_pieces[-1]
                for jj in range(2):
                    tc = th * 2 + jj
                    g = 4 * isc + tc
                    t_rb = E("sync", (lambda g_: lambda e: e.dma_start(
                        out=pool4[2][:], in_=x2s[g_]))(g),
                        waits=[free.get(("pool4", 2)), x2w_tok[isc]], inc=s_rb, inc_k=16)
                    t_nm, t_rs = emit_ln(
                        (lambda j_: lambda g_: dtm[j_][:, g_ * 512:g_ * 512 + 512])(jj),
                        4, [t_dtm_done], fset=True)
                    t_t4 = E("vector", (lambda j_: lambda e: e.tensor_scalar(
                        pool4[3][:], dtm[j_][:], nmf_s[:, 0:1], rstdf_s[:, 0:1],
                        ALU.add, ALU.mult))(jj),
                        waits=[t_nm, t_rs, free.get(("pool4", 3))], inc=s_dve)
                    free["rstdf"] = t_t4
                    t_o = E("vector", lambda e: e.tensor_add(
                        pool4[3][:], pool4[3][:], pool4[2][:]),
                        waits=[t_t4, t_rb], inc=s_dve)
                    free[("pool4", 2)] = t_o
                    t_st = E("sync", (lambda g_: lambda e: e.dma_start(
                        out=out_rows[g_], in_=pool4[3][:]))(g),
                        waits=[t_o], inc=s_out, inc_k=16)
                    free[("pool4", 3)] = t_st
                    t_out_sc.append(t_st)
                    yield 4500
            free["prs_hbm"] = [(x, x.n) for x in s_pb]
            free["sumpr"] = t_corr_last_all

        def drive():
            gw = gen_wcast()
            gm0 = gen_mamba(0)
            cw = cm0 = 0.0
            wdone = m0done = False
            while not (wdone and m0done):
                if (not m0done) and (wdone or cm0 <= cw):
                    try:
                        y = next(gm0)
                        if not isinstance(y, tuple):
                            cm0 += y
                    except StopIteration:
                        m0done = True
                else:
                    try:
                        cw += next(gw)
                    except StopIteration:
                        wdone = True
            def mamba_chain():
                for i in range(1, n_sc):
                    yield from gen_mamba(i)

            def ffn_chain():
                for i in range(n_sc):
                    yield from gen_ffn(i)

            gm = mamba_chain()
            gf = ffn_chain()
            cf = cm = 0.0
            fdone = mdone = False
            mneed = fneed = None
            while not (fdone and mdone):
                if mneed is not None and sh["xn3_epoch"] >= mneed:
                    mneed = None
                    if _PEG:
                        cm = cf = max(cm, cf)
                if fneed is not None and sh["mamba_done"] >= fneed:
                    fneed = None
                    if _PEG:
                        cm = cf = max(cm, cf)
                can_m = (not mdone) and mneed is None
                can_f = (not fdone) and fneed is None
                if not (can_m or can_f):
                    raise RuntimeError("emission deadlock")
                pick_m = can_m and ((not can_f) or cm <= cf)
                if pick_m:
                    try:
                        y = next(gm)
                        if isinstance(y, tuple):
                            mneed = y[1]
                        else:
                            cm += y
                    except StopIteration:
                        mdone = True
                else:
                    try:
                        y = next(gf)
                        if isinstance(y, tuple):
                            fneed = y[1]
                        else:
                            cf += y
                    except StopIteration:
                        fdone = True
        drive()

        import os as _os
        for _ in range(int(_os.environ.get("EXTRA_NOPS", "0"))):
            E("sync", lambda e: e.nop())
        if _os.environ.get("DBG_DUMP") == "1":
            dbg_specs = [("d_ytm0", ytm[0], BF16), ("d_xn1fm", xn1fm, BF16),
                         ("d_carry", carry, F32), ("d_negA", negA, F32),
                         ("d_delta0", delta[0], F32), ("d_dtm0", dtm[0], BF16),
                         ("d_x2", pool4[1], F32), ("d_bc16", bc16, BF16),
                         ("d_dtr", dtr_s, BF16), ("d_h0", h_t[0], F32),
                         ("d_da0", da_t[0], F32), ("d_dbu0", dbu_t[0], F32),
                         ("d_u0", u_t[0], F32), ("d_brep0", brep_all, BF16),
                         ("d_yb0", yb, BF16), ("d_xn3", xn3fm, BF16)]
            wfin = [(s_dve, s_dve.n), (s_act, s_act.n), (s_pe, s_pe.n),
                    (s_pl, s_pl.n), (s_out, s_out.n)]
            for nmx, tile, dtx in dbg_specs:
                hh = nc.dram_tensor(nmx, list(tile.shape), dtx,
                                    kind="ExternalOutput").ap()
                E("sync", (lambda h_, t_: lambda e: e.dma_start(out=h_, in_=t_[:]))(hh, tile),
                  waits=wfin, inc=s_out, inc_k=16)
        dma_sems = [sems[k] for k in sems
                    if k not in ("s_pe", "s_dve", "s_act", "s_pl")]
        E("sync", lambda e: e.nop(),
          waits=[(sm, sm.n) for sm in dma_sems])

        blk = es.enter_context(nc.Block())

        @blk.sync
        def _(sync):
            ctx.replay("sync", sync)

        @blk.gpsimd
        def _(g):
            ctx.replay("gpsimd", g)

        @blk.vector
        def _(v):
            ctx.replay("vector", v)

        @blk.scalar
        def _(s):
            ctx.replay("scalar", s)

        @blk.tensor
        def _(t):
            ctx.replay("tensor", t)

    return nc


def _prep_host(inputs, n_sc=4):
    bf = ml_dtypes.bfloat16
    f32 = np.float32
    W_xproj = np.asarray(inputs["L0_attn_W_xproj"], f32)
    W_dt = np.asarray(inputs["L0_attn_W_dt"], f32)
    b_dt = np.asarray(inputs["L0_attn_b_dt"], f32)
    A_log = np.asarray(inputs["L0_attn_A_log"], f32)
    D_par = np.asarray(inputs["L0_attn_D_param"], f32)
    wqg = np.asarray(inputs["L0_ffn_wq_gate"], np.int32)
    wqu = np.asarray(inputs["L0_ffn_wq_up"], np.int32)
    wqd = np.asarray(inputs["L0_ffn_wq_down"], np.int32)
    sgv = np.asarray(inputs["L0_ffn_s_gate"], f32)
    suv = np.asarray(inputs["L0_ffn_s_up"], f32)
    sdv = np.asarray(inputs["L0_ffn_s_down"], f32)

    common = {
        "wxt": np.ascontiguousarray(W_xproj.T),
        "wdt": np.ascontiguousarray(W_dt.T),
        "alog": np.ascontiguousarray(
            A_log.reshape(NDC, P, NST).transpose(1, 0, 2).reshape(P, 128)),
        "bdt": np.ascontiguousarray(b_dt.reshape(NDC, P).T),
        "dpar": np.ascontiguousarray(D_par.reshape(NDC, P).T),
        "wg": np.ascontiguousarray(wqg.T.reshape(D, NHC, P).transpose(1, 0, 2)),
        "wu": np.ascontiguousarray(wqu.T.reshape(D, NHC, P).transpose(1, 0, 2)),
        "wd": np.ascontiguousarray(wqd.T.reshape(NHC, P, D)),
        "sg": np.ascontiguousarray(sgv.reshape(NHC, P).T),
        "su": np.ascontiguousarray(suv.reshape(NHC, P).T),
        "sd": np.ascontiguousarray(sdv.reshape(NDC, P).T),
        "sel": np.kron(np.eye(16, dtype=f32), np.ones((1, P), f32)).astype(bf),
        "idb": np.eye(P, dtype=f32).astype(bf),
    }
    x = np.asarray(inputs["L0_attn_x"], f32)
    tok = 512 * n_sc
    maps = []
    for b in range(x.shape[0]):
        m = dict(common)
        m["x"] = np.ascontiguousarray(x[b, :tok])
        maps.append(m)
    return maps


def kernel(**inputs):
    n_sc = 4
    nc = build_nc(n_sc)
    maps = _prep_host(inputs, n_sc)
    res = run_bass_kernel_spmd(nc, maps, core_ids=list(range(len(maps))))
    outs = [np.asarray(r["out"], np.float32) for r in res.results]
    return np.stack(outs, axis=0)


# version 17 (final)
# speedup vs baseline: 1.0061x; 1.0061x over previous
import numpy as np
import ml_dtypes

import concourse.bass as bass
import concourse.mybir as mybir
from concourse.bass_utils import run_bass_kernel_spmd
from contextlib import ExitStack

F32 = mybir.dt.float32
BF16 = mybir.dt.bfloat16
I32 = mybir.dt.int32
AF = mybir.ActivationFunctionType
ALU = mybir.AluOpType

P = 128
D = 2048
NDC = 16
DTR = 256
NST = 8
HID = 5632
NHC = 44
KXP = 272
LN_EPS = 1e-5

ENGINES = ("sync", "gpsimd", "vector", "scalar", "tensor")
_PEG = True


class Sem:
    def __init__(self, handle, name):
        self.h = handle
        self.name = name
        self.n = 0


def _flat(waits):
    out = []

    def rec(w):
        if w is None:
            return
        if isinstance(w, tuple) and len(w) == 2 and isinstance(w[0], Sem):
            out.append(w)
            return
        for x in w:
            rec(x)
    rec(waits)
    return out


class Ctx:
    def __init__(self, paranoid=False, eng_sems=None):
        self.streams = {e: [] for e in ENGINES}
        self.waited = {e: {} for e in ENGINES}
        self.paranoid = paranoid
        self.eng_sems = eng_sems or {}

    def emit(self, eng, fn, waits=(), inc=None, inc_k=1, is_dma=False):
        waits = list(_flat(waits))
        se = self.eng_sems.get(eng)
        if self.paranoid and se is not None:
            waits.append((se, se.n))
            if inc is None and not is_dma:
                inc = se
                inc_k = 1
        pruned = []
        for s, v in waits:
            if self.waited[eng].get(s.name, 0) < v:
                pruned.append((s, v))
                self.waited[eng][s.name] = v
        tok = None
        if inc is not None:
            inc.n += inc_k
            tok = (inc, inc.n)
        self.streams[eng].append((pruned, fn, inc, inc_k))
        return tok

    def replay(self, eng_name, handle):
        for pruned, fn, inc, inc_k in self.streams[eng_name]:
            for s, v in pruned:
                handle.wait_ge(s.h, v)
            inst = fn(handle)
            if inc is not None:
                inst.then_inc(inc.h, inc_k)


def build_nc(n_sc=4, paranoid=False, limit=99):
    TOK = 512 * n_sc
    NG = TOK // P
    nc = bass.Bass("TRN2", target_bir_lowering=False,
                   detect_race_conditions=paranoid)

    x_h = nc.dram_tensor("x", [TOK, D], F32, kind="ExternalInput").ap()
    wxt_h = nc.dram_tensor("wxt", [D, KXP], F32, kind="ExternalInput").ap()
    wdt_h = nc.dram_tensor("wdt", [DTR, D], F32, kind="ExternalInput").ap()
    alog_h = nc.dram_tensor("alog", [P, 128], F32, kind="ExternalInput").ap()
    bdt_h = nc.dram_tensor("bdt", [P, NDC], F32, kind="ExternalInput").ap()
    dpar_h = nc.dram_tensor("dpar", [P, NDC], F32, kind="ExternalInput").ap()
    wg_h = nc.dram_tensor("wg", [NHC, D, P], I32, kind="ExternalInput").ap()
    wu_h = nc.dram_tensor("wu", [NHC, D, P], I32, kind="ExternalInput").ap()
    wd_h = nc.dram_tensor("wd", [NHC, P, D], I32, kind="ExternalInput").ap()
    sg_h = nc.dram_tensor("sg", [P, NHC], F32, kind="ExternalInput").ap()
    su_h = nc.dram_tensor("su", [P, NHC], F32, kind="ExternalInput").ap()
    sd_h = nc.dram_tensor("sd", [P, NDC], F32, kind="ExternalInput").ap()
    sel_h = nc.dram_tensor("sel", [16, D], BF16, kind="ExternalInput").ap()
    idb_h = nc.dram_tensor("idb", [P, P], BF16, kind="ExternalInput").ap()
    out_h = nc.dram_tensor("out", [TOK, D], F32, kind="ExternalOutput").ap()

    wgbf = nc.dram_tensor("wgbf", [NHC, P, D], BF16, kind="Internal").ap()
    wubf = nc.dram_tensor("wubf", [NHC, P, D], BF16, kind="Internal").ap()
    wdbf = nc.dram_tensor("wdbf", [NHC, P, D], BF16, kind="Internal").ap()
    x2s = nc.dram_tensor("x2s", [NG, P, D], F32, kind="Internal").ap()
    prs = nc.dram_tensor("prs", [NHC, P, 512], BF16, kind="Internal").ap()
    bch = nc.dram_tensor("bch", [16, 512], BF16, kind="Internal").ap()

    x_rows = x_h.rearrange("(g p) d -> g p d", p=P)
    out_rows = out_h.rearrange("(g p) d -> g p d", p=P)
    wxt_r = wxt_h.rearrange("(c p) k -> p c k", p=P)
    wdt_r = wdt_h.rearrange("(c p) d -> p c d", p=P)

    es = ExitStack()
    SB = lambda name, shp, dt=F32: es.enter_context(nc.sbuf_tensor(name, shp, dt))
    with es:
        wxw = SB("wxw", [P, NDC * KXP], BF16)
        negA = SB("negA", [P, 128])
        bdt_c = SB("bdt_c", [P, NDC])
        dpar_c = SB("dpar_c", [P, NDC])
        sg_s = SB("sg_s", [P, NHC]); su_s = SB("su_s", [P, NHC])
        nsg_s = SB("nsg_s", [P, NHC]); sd_s = SB("sd_s", [P, NDC])
        idb_s = SB("idb_s", [P, P], BF16)
        c128_s = SB("c128_s", [1, P])
        onec_s = SB("onec_s", [P, 1], BF16)
        bc16 = SB("bc16", [16, 512], BF16)
        sumx3_s = SB("sumx3_s", [1, 512])
        sumpr_s = SB("sumpr_s", [1, 512])
        xb = [SB(f"xb{k}", [P, D]) for k in range(2)]
        xn1tm = SB("xn1tm", [P, D], BF16)
        xn1fm = SB("xn1fm", [P, NDC * 512], BF16)
        dtr_s = SB("dtr_s", [P, 1024], BF16)
        delta = [SB(f"delta{k}", [P, 512]) for k in range(2)]
        spA = SB("spA", [P, 512]); spB = SB("spB", [P, 512])
        u_t = [SB("u0", [P, 512])] * 2
        brep_all = SB("brep_all", [P, 8 * 512], BF16)
        crep_all = SB("crep_all", [P, 8 * 512], BF16)
        da_t = [SB(f"da{k}", [P, 512]) for k in range(2)]
        dbu_t = [SB(f"dbu{k}", [P, 512]) for k in range(2)]
        h_t = [SB(f"h{k}", [P, 512]) for k in range(2)]
        hc_t = [SB(f"hct{k}", [P, 512], BF16) for k in range(2)]
        yb = SB("yb0", [P, 512], BF16)
        yfm = [SB(f"yfm{k}", [P, 512], BF16) for k in range(2)]
        ytm = [SB(f"ytm{k}", [P, D], BF16) for k in range(4)]
        carry = SB("carry", [P, 128])
        st6 = SB("st6", [P, 24]); mv = SB("mv", [P, 2])
        nm_s = SB("nm_s", [P, 1]); rstd_s = SB("rstd_s", [P, 1]); lnt = SB("lnt", [P, 1])
        eps_s = SB("eps_s", [P, 1])
        spc_d = SB("spc_d", [P, 1]); spc_a = SB("spc_a", [P, 1])
        st6f = SB("st6f", [P, 24]); mvf = SB("mvf", [P, 2])
        nmf_s = SB("nmf_s", [P, 1]); rstdf_s = SB("rstdf_s", [P, 1]); lntf = SB("lntf", [P, 1])
        pool4 = [SB(f"pool4_{k}", [P, D]) for k in range(4)]
        xn3fm = SB("xn3fm", [P, NDC * 512], BF16)
        wgt = [SB(f"wgt{k}", [P, D], BF16) for k in range(2)]
        wut = [SB(f"wut{k}", [P, D], BF16) for k in range(2)]
        wdd = [SB(f"wdd{k}", [P, D], BF16) for k in range(2)]
        wddh = [wdd[0][:, 0:1024], wdd[0][:, 1024:2048],
                wdd[1][:, 0:1024], wdd[1][:, 1024:2048]]
        tclA = [SB("tclA0", [P, 512])]
        e1B = [SB("e1B0", [P, 512])]
        u1C = [SB("u1C0", [P, 512])]
        upt = [SB(f"upt{k}", [P, 512], BF16) for k in range(2)]
        gbt = [SB(f"gbt{k}", [P, 512], BF16) for k in range(2)]
        prt = [SB(f"prt{k}", [P, 512], BF16) for k in range(2)]
        prb = [SB(f"prb{k}", [P, 256], BF16) for k in range(2)]
        dfm = [SB(f"dfm{k}", [P, 256], BF16) for k in range(2)]
        dtm = [SB(f"dtm{k}", [P, D], BF16) for k in range(2)]

        ps = es.enter_context(nc.psum_tensor("ps", [P, 4096], F32))
        tbf = ps[:, 3072:3584].bitcast(BF16)

        sems = {}
        for nm in ("s_ini", "s_scr", "s_pr", "s_x2w", "s_rb",
                   "s_out", "s_cx", "s_pe", "s_dve", "s_act", "s_pl",
                   "s_xb0", "s_xb1", "s_cg0", "s_cg1", "s_cu0", "s_cu1",
                   "s_cd0", "s_cd1", "s_pb0", "s_pb1", "s_pb2",
                   "s_rd2", "s_rd3", "s_bch", "s_bc2",
                   "s_sg0", "s_sg1", "s_su0", "s_su1", "s_sd0", "s_sd1",
                   "s_pw0", "s_pw1",
                   "s_rg0", "s_rg1", "s_ru0", "s_ru1", "s_rd0", "s_rd1"):
            sems[nm] = Sem(es.enter_context(nc.semaphore(nm)), nm)
        s_ini, s_scr, s_pr, s_x2w, s_rb, s_out, s_cx, s_pe, s_dve, s_act, s_pl = (
            sems[k] for k in ("s_ini", "s_scr", "s_pr", "s_x2w", "s_rb",
                              "s_out", "s_cx", "s_pe", "s_dve", "s_act", "s_pl"))
        s_xb = [sems["s_xb0"], sems["s_xb1"]]
        s_cg = [sems["s_cg0"], sems["s_cg1"]]
        s_cu = [sems["s_cu0"], sems["s_cu1"]]
        s_cd = [sems["s_cd0"], sems["s_cd1"]]
        s_pb = [sems["s_pb0"], sems["s_pb1"], sems["s_pb2"]]
        s_sg = [sems["s_sg0"], sems["s_sg1"]]
        s_su2 = [sems["s_su0"], sems["s_su1"]]
        s_sd2 = [sems["s_sd0"], sems["s_sd1"]]
        s_pw = [sems["s_pw0"], sems["s_pw1"]]
        s_rg = [sems["s_rg0"], sems["s_rg1"]]
        s_ru = [sems["s_ru0"], sems["s_ru1"]]
        s_rd = [sems["s_rd0"], sems["s_rd1"], sems["s_rd2"], sems["s_rd3"]]
        s_bch = sems["s_bch"]; s_bc2 = sems["s_bc2"]

        ctx = Ctx(paranoid=paranoid,
                  eng_sems={"vector": s_dve, "scalar": s_act,
                            "tensor": s_pe, "gpsimd": s_pl})
        E = ctx.emit

        def gate_ps(hc):
            o = (hc % 2) * 512
            return ps[:, o:o + 512]

        def up_ps(hc):
            o = 1024 + (hc % 2) * 512
            return ps[:, o:o + 512]

        def mm_ps(j):
            o = 2048 + (j % 2) * 512
            return ps[:, o:o + 512]

        tbf4 = ps[:, 2048:2560].bitcast(BF16)
        tbf5 = ps[:, 2560:3072].bitcast(BF16)
        tr_views = {4: tbf4, 5: tbf5}
        tr_st = {"k": 0, "ev": {4: None, 5: None, 6: None}}

        def emit_tr(src_ap, dst_ap, waits, banks=(6,)):
            k = tr_st["k"]; tr_st["k"] += 1
            bank = banks[k % len(banks)]
            pos = (k // len(banks)) % 4
            view = tbf if bank == 6 else tr_views[bank]
            slot = view[:, pos * 128:(pos + 1) * 128]
            t_tr = E("tensor",
                     (lambda sl_, sa_: lambda e: e.transpose(sl_, sa_, idb_s[:]))(slot, src_ap),
                     waits=list(_flat(waits)) + list(_flat(tr_st["ev"][bank])), inc=s_pe)
            t_ev = E("vector",
                     (lambda da_, sl_: lambda e: e.tensor_copy(da_, sl_))(dst_ap, slot),
                     waits=[t_tr], inc=s_dve)
            tr_st["ev"][bank] = t_ev
            return t_ev

        def tr_rotate_begin():
            tr_st["k"] = 0
            tr_st["ev"][4] = [tr_st["ev"][4], mmslot_free.get(0)]
            tr_st["ev"][5] = [tr_st["ev"][5], mmslot_free.get(1)]

        def tr_rotate_end():
            mmslot_free[0] = [mmslot_free.get(0), tr_st["ev"][4]]
            mmslot_free[1] = [mmslot_free.get(1), tr_st["ev"][5]]

        def dn_ps(dc):
            return ps[:, dc * 256:dc * 256 + 256]

        sums_ps = ps[0:1, 3584:4096]

        free = {}
        tbf_free = {}
        mmslot_free = {}

        def dma(dst, src):
            return lambda e: e.dma_start(out=dst, in_=src)

        E("sync", dma(negA[:], alog_h), inc=s_ini, inc_k=16)
        E("sync", dma(bdt_c[:], bdt_h), inc=s_ini, inc_k=16)
        E("sync", dma(dpar_c[:], dpar_h), inc=s_ini, inc_k=16)
        E("sync", dma(sg_s[:], sg_h), inc=s_ini, inc_k=16)
        E("sync", dma(su_s[:], su_h), inc=s_ini, inc_k=16)
        E("sync", dma(sd_s[:], sd_h), inc=s_ini, inc_k=16)
        E("sync", dma(idb_s[:], idb_h), inc=s_ini, inc_k=16)
        t_init = (s_ini, s_ini.n)
        t_eA = E("scalar", lambda e: e.activation(negA[:], negA[:], AF.Exp),
                 waits=[t_init], inc=s_act)
        t_negA = E("vector", lambda e: e.tensor_scalar_mul(negA[:], negA[:], -1.0),
                   waits=[t_eA], inc=s_dve)
        t_nsg = E("vector", lambda e: e.tensor_scalar_mul(nsg_s[:], sg_s[:], -1.0),
                  waits=[t_init], inc=s_dve)
        E("vector", lambda e: e.memset(carry[:], 0.0), inc=s_dve)
        t_eps = E("vector", lambda e: e.memset(eps_s[:], LN_EPS), inc=s_dve)
        t_c128 = E("vector", lambda e: e.memset(c128_s[:], -128.0), inc=s_dve)
        t_onec = E("vector", lambda e: e.memset(onec_s[:], 1.0), inc=s_dve)

        def emit_ln(src_ap_fn, n_groups, extra_first_waits, fset=False):
            ST, MV, NM, RS, LT = ((st6f, mvf, nmf_s, rstdf_s, lntf) if fset
                                  else (st6, mv, nm_s, rstd_s, lnt))
            mk = "mvf" if fset else "mv"
            rk = "rstdf" if fset else "rstd"
            w_mv = free.get(mk)
            for g in range(n_groups):
                E("vector",
                  (lambda g_: lambda e: e.bn_stats(ST[:, g_ * 6:g_ * 6 + 6], src_ap_fn(g_)))(g),
                  waits=(extra_first_waits if g == 0 else []), inc=s_dve)
            E("vector", lambda e: e.tensor_copy(spc_d[:], ST[:, 0:1]), inc=s_dve)
            E("vector", lambda e: e.bn_aggr(MV[:], ST[:, 0:6 * n_groups]),
              waits=[w_mv], inc=s_dve)
            t_ag = E("vector", lambda e: e.tensor_copy(spc_d[:], MV[:, 0:1]),
                     inc=s_dve)
            t_nm = E("vector", lambda e: e.tensor_scalar_mul(NM[:], MV[:, 0:1], -1.0),
                     waits=[t_ag], inc=s_dve)
            t_ln = E("scalar", lambda e: e.activation(LT[:], MV[:, 1:2], AF.Ln, bias=eps_s[:, 0:1]),
                     waits=[t_ag, t_eps], inc=s_act)
            E("scalar", lambda e: e.activation(RS[:], LT[:], AF.Exp, scale=-0.5),
              waits=[t_ln, free.get(rk)], inc=s_act)
            t_rs = E("scalar", lambda e: e.copy(spc_a[:], RS[:]),
                     inc=s_act)
            free[mk] = t_ln
            return t_nm, t_rs

        tok_xn3_free = None
        tok_x2w = [None] * NG
        xn3_tok = {}
        x2w_tok = {}
        sh = {"xn3_epoch": -1, "mamba_done": -1}
        scr_tok_g = [None] * NHC
        scr_tok_u = [None] * NHC
        scr_tok_d = [None] * NHC
        dn_ev = [None] * 8

        def gen_wcast():
            for hc in range(NHC):
                wsl = hc % 2
                t_lg = E("gpsimd", (lambda h_, s_: lambda e: e.dma_start(
                    out=wgt[s_].rearrange("p (c h) -> p c h", c=NDC),
                    in_=wg_h[h_].rearrange("(c p) h -> p c h", p=P)))(hc, wsl),
                    waits=[free.get(("wgt", wsl))], inc=s_cg[wsl], inc_k=16)
                t_sgw = E("sync", (lambda h_, s_: lambda e: e.dma_start(
                    out=wgbf[h_], in_=wgt[s_][:]))(hc, wsl), waits=[t_lg],
                    inc=s_sg[wsl], inc_k=16)
                free[("wgt", wsl)] = t_sgw
                scr_tok_g[hc] = t_sgw
                yield 2600
                t_lu = E("gpsimd", (lambda h_, s_: lambda e: e.dma_start(
                    out=wut[s_].rearrange("p (c h) -> p c h", c=NDC),
                    in_=wu_h[h_].rearrange("(c p) h -> p c h", p=P)))(hc, wsl),
                    waits=[free.get(("wut", wsl))], inc=s_cu[wsl], inc_k=16)
                t_suw = E("sync", (lambda h_, s_: lambda e: e.dma_start(
                    out=wubf[h_], in_=wut[s_][:]))(hc, wsl), waits=[t_lu],
                    inc=s_su2[wsl], inc_k=16)
                free[("wut", wsl)] = t_suw
                scr_tok_u[hc] = t_suw
                yield 2600
                t_ldc = E("gpsimd", (lambda h_, s_: lambda e: e.dma_start(
                    out=wdd[s_][:], in_=wd_h[h_]))(hc, wsl),
                    waits=[free.get(("wdd", wsl))], inc=s_cd[wsl], inc_k=16)
                t_sdw = E("sync", (lambda h_, s_: lambda e: e.dma_start(
                    out=wdbf[h_], in_=wdd[s_][:]))(hc, wsl), waits=[t_ldc],
                    inc=s_sd2[wsl], inc_k=16)
                free[("wdd", wsl)] = t_sdw
                scr_tok_d[hc] = t_sdw
                yield 2600
            for q in range(4):
                free[("wddh", q)] = free.get(("wdd", q // 2))

        def gen_mamba(isc):
            nonlocal tok_xn3_free
            tok_xn1tm_free = free.get("xn1tm")
            tr_rotate_begin()
            for tc in range(4):
                g = 4 * isc + tc
                sl = g % 2
                t_xin = E("sync", (lambda g_, s_: lambda e: e.dma_start(
                    out=xb[s_][:], in_=x_rows[g_]))(g, sl),
                    waits=[free.get(("xb", sl))], inc=s_xb[sl], inc_k=16)
                t_nm, t_rs = emit_ln(
                    (lambda s_: lambda g_: xb[s_][:, g_ * 512:g_ * 512 + 512])(sl),
                    4, [t_xin])
                t_xn1 = E("vector", (lambda s_: lambda e: e.tensor_scalar(
                    xn1tm[:], xb[s_][:], nm_s[:, 0:1], rstd_s[:, 0:1], ALU.add, ALU.mult))(sl),
                    waits=[t_nm, t_rs, tok_xn1tm_free], inc=s_dve)
                t_xn1 = E("vector", lambda e: e.tensor_copy(lnt[:], xn1tm[:, 0:1]),
                          inc=s_dve)
                free["rstd"] = t_xn1
                free[("xb", sl)] = t_xn1
                yield 4500
                last_ev = None
                for dc in range(NDC):
                    wts = [t_xn1, t_init]
                    last_ev = emit_tr(
                        xn1tm[:, dc * 128:dc * 128 + 128],
                        xn1fm[:, dc * 512 + tc * 128:dc * 512 + tc * 128 + 128],
                        wts, banks=(6, 4, 5))
                    if dc % 2 == 1:
                        yield 650
                tok_xn1tm_free = last_ev
            tr_rotate_end()
            free["xn1tm"] = tok_xn1tm_free
            t_xn1fm_done = tok_xn1tm_free

            if limit < 2:
                return
            t_wx = E("gpsimd", lambda e: e.dma_start(
                out=wxw.rearrange("p (c k) -> p c k", c=NDC), in_=wxt_r),
                waits=[free.get("wxw")], inc=s_cx, inc_k=16)
            evs = []
            for m in range(3):
                mw = 128 if m < 2 else 16
                c0 = m * 128
                j = m % 2
                wts = [t_wx, t_xn1fm_done, mmslot_free.get(j)]
                if m == 2:
                    wts.append(evs[0])
                tprev = None
                for dc in range(NDC):
                    tprev = E("tensor", (lambda j_, dc_, c0_, mw_: lambda e: e.matmul(
                        mm_ps(j_)[0:mw_, :],
                        wxw[:, dc_ * KXP + c0_:dc_ * KXP + c0_ + mw_],
                        xn1fm[:, dc_ * 512:dc_ * 512 + 512],
                        start=(dc_ == 0), stop=(dc_ == NDC - 1)))(j, dc, c0, mw),
                        waits=(wts if dc == 0 else []),
                        inc=(s_pe if dc == NDC - 1 else None))
                if m < 2:
                    tev = E("scalar", (lambda j_, m_: lambda e: e.copy(
                        dtr_s[:, m_ * 512:m_ * 512 + 512], mm_ps(j_)[:]))(j, m),
                        waits=[tprev], inc=s_act)
                else:
                    tev = E("scalar", (lambda j_: lambda e: e.copy(
                        bc16[:], mm_ps(j_)[0:16, :]))(j),
                        waits=[tprev], inc=s_act)
                mmslot_free[j] = tev
                evs.append(tev)
                yield 2000
            t_dtr01 = evs[1]
            t_bc = evs[2]
            t_m2mm = (s_pe, s_pe.n)
            t_bchw = E("sync", lambda e: e.dma_start(out=bch, in_=bc16[:]),
                       waits=[t_bc, free.get("bch")], inc=s_bch, inc_k=16)
            for n16 in range(16):
                dst = brep_all if n16 < 8 else crep_all
                E("sync", (lambda d_, n_: lambda e: e.dma_start(
                    out=d_[:, (n_ % 8) * 512:(n_ % 8) * 512 + 512],
                    in_=bch[n_:n_ + 1, :].broadcast_to([P, 512])))(dst, n16),
                    waits=([t_bchw, free.get("bcast_bufs")] if n16 == 0 else []),
                    inc=s_bc2, inc_k=16)
            t_bcast = (s_bc2, s_bc2.n)
            free["bch"] = t_bcast
            yield 2000
            t_wd = E("gpsimd", lambda e: e.dma_start(
                out=wxw[:, 0:2 * D].rearrange("p (c d) -> p c d", c=2), in_=wdt_r),
                waits=[t_m2mm], inc=s_cx, inc_k=16)
            wdt_view = wxw[:, 0:2 * D]

            if limit < 3:
                return
            t_last_ytr = None
            t_db = t_hc = None
            for dc in range(NDC):
                dsl = dc % 2
                E("tensor", (lambda dc_: lambda e: e.matmul(
                    mm_ps(0), wdt_view[:, dc_ * 128:dc_ * 128 + 128],
                    dtr_s[:, 0:512], start=True, stop=False))(dc),
                    waits=[t_wd, evs[0], t_dtr01, mmslot_free.get(0)])
                t_dmm = E("tensor", (lambda dc_: lambda e: e.matmul(
                    mm_ps(0), wdt_view[:, D + dc_ * 128:D + dc_ * 128 + 128],
                    dtr_s[:, 512:1024], start=False, stop=True))(dc), inc=s_pe)
                t1 = E("scalar", (lambda dc_: lambda e: e.activation(
                    spA[:], mm_ps(0), AF.Relu, bias=bdt_c[:, dc_:dc_ + 1]))(dc),
                    waits=[t_dmm, free.get("spA")], inc=s_act)
                t2 = E("scalar", (lambda dc_: lambda e: e.activation(
                    spB[:], mm_ps(0), AF.Abs, bias=bdt_c[:, dc_:dc_ + 1]))(dc),
                    waits=[free.get("spB")], inc=s_act)
                mmslot_free[0] = t2
                t3 = E("scalar", (lambda d_: lambda e: e.activation(
                    delta[d_][:], spB[:], AF.Exp, scale=-1.0))(dsl),
                    waits=[free.get(("delta", dsl))], inc=s_act)
                t4 = E("scalar", (lambda d_: lambda e: e.activation(
                    spB[:], delta[d_][:], AF.Ln, bias=1.0))(dsl), inc=s_act)
                t_delta = E("vector", (lambda d_: lambda e: e.tensor_add(
                    delta[d_][:], spA[:], spB[:]))(dsl),
                    waits=[t4], inc=s_dve)
                free["spA"] = t_delta
                free["spB"] = t_delta
                t_u = E("gpsimd", (lambda d_, dc_: lambda e: e.tensor_mul(
                    u_t[d_][:], delta[d_][:], xn1fm[:, dc_ * 512:dc_ * 512 + 512]))(dsl, dc),
                    waits=[t_delta], inc=s_pl)
                t_ts = E("vector", (lambda dc_: lambda e: e.tensor_scalar(
                    yb[:], xn1fm[:, dc_ * 512:dc_ * 512 + 512],
                    dpar_c[:, dc_:dc_ + 1], None, ALU.mult))(dc),
                    waits=[t_xn1fm_done, free.get("yb")], inc=s_dve)
                t_ts = E("vector", lambda e: e.tensor_copy(spc_d[:], yb[:, 0:1]),
                         inc=s_dve)
                pend_ym = ("start", [t_ts, mmslot_free.get(1), t_init])
                yield 3500
                last_da = None
                t_ym = None
                pend_carry = None

                def flush_ym():
                    nonlocal pend_ym, t_ym
                    if pend_ym is None:
                        return
                    kind, wts_ = pend_ym
                    if kind == "start":
                        t_ym = E("tensor", lambda e: e.matmul(
                            mm_ps(1), idb_s[:], yb[:], start=True, stop=False),
                            waits=wts_, inc=s_pe)
                        free["yb"] = t_ym
                    else:
                        n_, s_ = kind
                        t_ym = E("tensor", (lambda ss, nn: lambda e: e.matmul(
                            mm_ps(1), idb_s[:], hc_t[ss][:],
                            start=False, stop=(nn == NST - 1)))(s_, n_),
                            waits=wts_, inc=s_pe)
                        free[("hc", s_)] = t_ym
                    pend_ym = None

                def emit_dbu(n_):
                    return E("gpsimd", (lambda d_, s_, nn: lambda e: e.tensor_mul(
                        dbu_t[s_][:], u_t[d_][:],
                        brep_all[:, nn * 512:nn * 512 + 512]))(dsl, n_ % 2, n_),
                        waits=[t_u, t_bcast, free.get(("dbu", n_ % 2))], inc=s_pl)

                t_db_pre = emit_dbu(0)
                for n in range(NST):
                    nsl = n % 2
                    flush_ym()
                    t_da = E("scalar", (lambda d_, s_, dc_, n_: lambda e: e.activation(
                        da_t[s_][:], delta[d_][:], AF.Exp,
                        scale=negA[:, dc_ * 8 + n_:dc_ * 8 + n_ + 1]))(dsl, nsl, dc, n),
                        waits=[t_negA, free.get(("da", nsl))]
                              + ([t_delta] if n == 0 else []), inc=s_act)
                    last_da = t_da
                    t_db = t_db_pre
                    if n + 1 < NST:
                        t_db_pre = emit_dbu(n + 1)
                    idx = dc * 8 + n
                    t_sc = E("vector", (lambda s_, i_: lambda e: e.tensor_tensor_scan(
                        h_t[s_][:], da_t[s_][:], dbu_t[s_][:], carry[:, i_:i_ + 1],
                        ALU.mult, ALU.add))(nsl, idx),
                        waits=[t_da, t_db, free.get(("h", nsl))], inc=s_dve)
                    free[("da", nsl)] = t_sc
                    free[("dbu", nsl)] = t_sc
                    if pend_carry is not None:
                        ps_, pi_ = pend_carry
                        tcc = E("vector", (lambda s_, i_: lambda e: e.tensor_copy(
                            carry[:, i_:i_ + 1], h_t[s_][:, 511:512]))(ps_, pi_),
                            inc=s_dve)
                        free[("h", ps_)] = tcc
                    pend_carry = (nsl, idx)
                    use_pool = (n % 2 == 0)
                    t_hc = E("gpsimd" if use_pool else "vector",
                             (lambda s_, n_: lambda e: e.tensor_mul(
                                 hc_t[s_][:], h_t[s_][:],
                                 crep_all[:, n_ * 512:n_ * 512 + 512]))(nsl, n),
                             waits=[t_sc, t_bcast, free.get(("hc", nsl))],
                             inc=(s_pl if use_pool else s_dve))
                    if not use_pool:
                        t_hc = E("vector", (lambda s_: lambda e: e.tensor_copy(
                            spc_d[:], hc_t[s_][:, 0:1]))(nsl), inc=s_dve)
                    pend_ym = ((n, nsl), [t_hc])
                    yield 2000
                flush_ym()
                ps_, pi_ = pend_carry
                tcc = E("vector", (lambda s_, i_: lambda e: e.tensor_copy(
                    carry[:, i_:i_ + 1], h_t[s_][:, 511:512]))(ps_, pi_),
                    inc=s_dve)
                free[("h", ps_)] = tcc
                free[("delta", dsl)] = [last_da, t_u]
                t_yf = E("scalar", (lambda f_: lambda e: e.activation(
                    yfm[f_][:], mm_ps(1), AF.Copy))(dsl),
                    waits=[t_ym, free.get(("yfm", dsl))], inc=s_act)
                t_yfs = E("scalar", (lambda f_: lambda e: e.copy(
                    spc_a[:], yfm[f_][:, 0:1]))(dsl), inc=s_act)
                mmslot_free[1] = t_yf
                for jj in range(4):
                    t_ev = emit_tr(
                        yfm[dsl][:, jj * 128:jj * 128 + 128],
                        ytm[jj][:, dc * 128:dc * 128 + 128],
                        [t_yf, t_yfs])
                    if jj == 1:
                        yield 650
                    if jj == 3:
                        free[("yfm", dsl)] = t_ev
                        t_last_ytr = t_ev
                yield 650
            free["bcast_bufs"] = [t_db, t_hc]
            free["wxw"] = t_dmm

            if limit < 4:
                return
            while sh["xn3_epoch"] < isc - 1:
                yield ("blk", isc - 1)
            tok_xn1tm_free = free.get("xn1tm")
            tr_rotate_begin()
            for tc in range(4):
                g = 4 * isc + tc
                sl = g % 2
                t_xin = E("sync", (lambda g_, s_: lambda e: e.dma_start(
                    out=xb[s_][:], in_=x_rows[g_]))(g, sl),
                    waits=[free.get(("xb", sl))], inc=s_xb[sl], inc_k=16)
                t_nm, t_rs = emit_ln(
                    (lambda t_: lambda g_: ytm[t_][:, g_ * 512:g_ * 512 + 512])(tc),
                    4, [t_last_ytr])
                t_t2 = E("vector", (lambda t_: lambda e: e.tensor_scalar(
                    pool4[0][:], ytm[t_][:], nm_s[:, 0:1], rstd_s[:, 0:1],
                    ALU.add, ALU.mult))(tc),
                    waits=[t_nm, t_rs, free.get(("pool4", 0))], inc=s_dve)
                free["rstd"] = t_t2
                t_x2 = E("vector", (lambda s_: lambda e: e.tensor_add(
                    pool4[1][:], xb[s_][:], pool4[0][:]))(sl),
                    waits=[t_xin, t_t2, free.get(("pool4", 1))], inc=s_dve)
                free[("pool4", 0)] = t_x2
                free[("xb", sl)] = t_x2
                t_w2 = E("sync", (lambda g_: lambda e: e.dma_start(
                    out=x2s[g_], in_=pool4[1][:]))(g), waits=[t_x2], inc=s_x2w, inc_k=16)
                tok_x2w[g] = t_w2
                yield 5000
                t_nm, t_rs = emit_ln(
                    lambda g_: pool4[1][:, g_ * 512:g_ * 512 + 512], 4, [t_x2])
                t_xn3 = E("vector", lambda e: e.tensor_scalar(
                    xn1tm[:], pool4[1][:], nm_s[:, 0:1], rstd_s[:, 0:1], ALU.add, ALU.mult),
                    waits=[t_nm, t_rs, tok_xn1tm_free], inc=s_dve)
                t_xn3 = E("vector", lambda e: e.tensor_copy(lnt[:], xn1tm[:, 0:1]),
                          inc=s_dve)
                free["rstd"] = t_xn3
                free[("pool4", 1)] = [t_xn3, t_w2]
                yield 4500
                last_ev = None
                for dc in range(NDC):
                    last_ev = emit_tr(
                        xn1tm[:, dc * 128:dc * 128 + 128],
                        xn3fm[:, dc * 512 + tc * 128:dc * 512 + tc * 128 + 128],
                        [t_xn3, tok_xn3_free], banks=(6, 4, 5))
                    tok_xn3_free = None
                    if dc % 2 == 1:
                        yield 650
                tok_xn1tm_free = last_ev
            tr_rotate_end()
            free["xn1tm"] = tok_xn1tm_free
            xn3_tok[isc] = tok_xn1tm_free
            x2w_tok[isc] = (s_x2w, s_x2w.n)
            sh["mamba_done"] = isc

        def gen_ffn(isc):
            nonlocal tok_xn3_free
            if limit < 5:
                return
            while sh["mamba_done"] < isc:
                yield ("blk_m", isc)
            tprev = None
            for dc in range(NDC):
                tprev = E("tensor", (lambda dc_: lambda e: e.matmul(
                    sums_ps[:, 0:512], onec_s[:], xn3fm[:, dc_ * 512:dc_ * 512 + 512],
                    start=(dc_ == 0), stop=(dc_ == NDC - 1)))(dc),
                    waits=([xn3_tok[isc], t_onec, free.get("sums")] if dc == 0 else []),
                    inc=(s_pe if dc == NDC - 1 else None))
            E("scalar", lambda e: e.copy(sumx3_s[:], sums_ps[:, 0:512]),
              waits=[tprev, free.get("sumx3")], inc=s_act)
            t_sx3 = E("scalar", lambda e: e.copy(spc_a[0:1, 0:1], sumx3_s[0:1, 0:1]),
                      inc=s_act)
            free["sums"] = t_sx3
            yield 1500

            t_pr_wr = [None] * NHC
            t_last_gu = None
            t_spr_last = None
            pend_spr = None
            pend_wpr = None
            for hc in range(NHC if limit >= 42 else 0):
                wsl = hc % 2
                t_lg = E("sync", (lambda h_, s_: lambda e: e.dma_start(
                    out=wgt[s_][:], in_=wgbf[h_]))(hc, wsl),
                    waits=[free.get(("wgt", wsl)), scr_tok_g[hc]],
                    inc=s_rg[wsl], inc_k=16)
                t_lu = E("sync", (lambda h_, s_: lambda e: e.dma_start(
                    out=wut[s_][:], in_=wubf[h_]))(hc, wsl),
                    waits=[free.get(("wut", wsl)), scr_tok_u[hc]],
                    inc=s_ru[wsl], inc_k=16)
                t_sgw = t_suw = None
                if pend_wpr is not None:
                    h_p, s_p, w_p = pend_wpr
                    t_wpr = E("sync", (lambda h_, s_: lambda e: e.dma_start(
                        out=prs[h_], in_=prt[s_][:]))(h_p, s_p),
                        waits=w_p, inc=s_pw[s_p], inc_k=16)
                    free[("prt", s_p)] = [free.get(("prt", s_p)), t_wpr]
                    pend_wpr = None
                if limit < 43:
                    free[("wgt", wsl)] = t_sgw if t_sgw is not None else t_lg
                    free[("wut", wsl)] = t_suw if t_suw is not None else t_lu
                    continue
                use_corr = limit != 441
                if limit >= 460 and limit <= 469 and hc >= (limit - 460):
                    free[("wgt", wsl)] = t_sgw if t_sgw is not None else t_lg
                    free[("wut", wsl)] = t_suw if t_suw is not None else t_lu
                    continue
                lhs_fn = (lambda s_, dc_: wxw[:, dc_ * KXP:dc_ * KXP + 128]) if limit == 445 \
                    else (lambda s_, dc_: wgt[s_][:, dc_ * 128:dc_ * 128 + 128])
                rhs_fn = (lambda dc_: xn1fm[:, dc_ * 512:dc_ * 512 + 512]) if limit == 444 \
                    else (lambda dc_: xn3fm[:, dc_ * 512:dc_ * 512 + 512])
                for dc in range(NDC):
                    E("tensor", (lambda h_, s_, dc_: lambda e: e.matmul(
                        gate_ps(h_), lhs_fn(s_, dc_), rhs_fn(dc_),
                        start=(dc_ == 0),
                        stop=(not use_corr and dc_ == NDC - 1)))(hc, wsl, dc),
                        waits=([t_lg, t_sx3, t_c128, free.get(("gps", hc % 2))]
                               if dc == 0 else []),
                        inc=(s_pe if (not use_corr and dc == NDC - 1) else None))
                    if dc == 7:
                        yield 2400
                if use_corr:
                    t_gmm = E("tensor", (lambda h_: lambda e: e.matmul(
                        gate_ps(h_), c128_s[:], sumx3_s[:], start=False, stop=True))(hc),
                        inc=s_pe)
                else:
                    t_gmm = (s_pe, s_pe.n)
                yield 2800
                if pend_spr is not None:
                    h_p, s_p, w_p = pend_spr
                    t_spr = E("tensor", (lambda h_, s_: lambda e: e.matmul(
                        sums_ps[:, 0:512], onec_s[:], prt[s_][:],
                        start=(h_ == 0), stop=(h_ == NHC - 1)))(h_p, s_p),
                        waits=w_p, inc=s_pe)
                    t_spr_last = t_spr
                    free[("prt", s_p)] = [free.get(("prt", s_p)), t_spr]
                    pend_spr = None
                for dc in range(NDC if (limit >= 44 and not (440 <= limit <= 469)) else 0):
                    E("tensor", (lambda h_, s_, dc_: lambda e: e.matmul(
                        up_ps(h_), wut[s_][:, dc_ * 128:dc_ * 128 + 128],
                        xn3fm[:, dc_ * 512:dc_ * 512 + 512],
                        start=(dc_ == 0), stop=False))(hc, wsl, dc),
                        waits=([t_lu, free.get(("ups", hc % 2))] if dc == 0 else []))
                    if dc == 7:
                        yield 2400
                if limit >= 44 and not (440 <= limit <= 469):
                    t_umm = E("tensor", (lambda h_: lambda e: e.matmul(
                        up_ps(h_), c128_s[:], sumx3_s[:], start=False, stop=True))(hc),
                        inc=s_pe)
                else:
                    t_umm = t_gmm
                t_last_gu = t_umm
                free[("wgt", wsl)] = [t_gmm, t_sgw]
                free[("wut", wsl)] = [t_umm, t_suw]
                esl = hc % 2
                t_u1 = E("scalar", (lambda h_: lambda e: e.activation(
                    u1C[0][:], gate_ps(h_), AF.Copy, scale=sg_s[:, h_:h_ + 1]))(hc),
                    waits=[t_gmm, free.get("u1C")], inc=s_act)
                free[("gps", hc % 2)] = t_u1
                if limit < 45 or (440 <= limit <= 469):
                    free[("ups", hc % 2)] = t_umm
                    continue
                t_e1 = E("scalar", lambda e: e.activation(
                    e1B[0][:], u1C[0][:], AF.Exp, scale=-1.0),
                    waits=[t_u1, free.get("e1B")], inc=s_act)
                t_l1 = E("scalar", lambda e: e.activation(
                    tclA[0][:], e1B[0][:], AF.Ln, bias=1.0), inc=s_act)
                t_s1 = E("scalar", lambda e: e.activation(
                    e1B[0][:], tclA[0][:], AF.Exp, scale=-1.0), inc=s_act)
                free["tclA"] = t_s1
                t_gb = E("gpsimd", (lambda s_: lambda e: e.tensor_mul(
                    gbt[s_][:], u1C[0][:], e1B[0][:]))(esl),
                    waits=[t_s1, t_u1, free.get(("gbt", esl))], inc=s_pl)
                free["e1B"] = t_gb
                free["u1C"] = t_gb
                t_up = E("vector", (lambda h_, s_: lambda e: e.tensor_scalar(
                    upt[s_][:], up_ps(h_), su_s[:, h_:h_ + 1], None, ALU.mult))(hc, esl),
                    waits=[t_umm, free.get(("upt", esl))], inc=s_dve)
                free[("ups", hc % 2)] = t_up
                if limit < 46 or (440 <= limit <= 469):
                    continue
                t_pr = E("vector", (lambda s_: lambda e: e.tensor_mul(
                    prt[s_][:], gbt[s_][:], upt[s_][:]))(esl),
                    waits=[t_gb, t_up, free.get(("prt", esl))], inc=s_dve)
                free[("gbt", esl)] = t_pr
                free[("upt", esl)] = t_pr
                pend_spr = (hc, esl, [t_pr] + ([free.get("sums")] if hc == 0 else []))
                pend_wpr = (hc, esl,
                            [t_pr] + ([free.get("prs_hbm")] if hc == 0 else []))
                free[("prt", esl)] = t_pr
                yield 2800
            if pend_wpr is not None:
                h_p, s_p, w_p = pend_wpr
                t_wpr = E("sync", (lambda h_, s_: lambda e: e.dma_start(
                    out=prs[h_], in_=prt[s_][:]))(h_p, s_p),
                    waits=w_p, inc=s_pw[s_p], inc_k=16)
                free[("prt", s_p)] = [free.get(("prt", s_p)), t_wpr]
                pend_wpr = None
            if pend_spr is not None:
                h_p, s_p, w_p = pend_spr
                t_spr = E("tensor", (lambda h_, s_: lambda e: e.matmul(
                    sums_ps[:, 0:512], onec_s[:], prt[s_][:],
                    start=(h_ == 0), stop=(h_ == NHC - 1)))(h_p, s_p),
                    waits=w_p, inc=s_pe)
                t_spr_last = t_spr
                free[("prt", s_p)] = [free.get(("prt", s_p)), t_spr]
                pend_spr = None
            tok_xn3_free = t_last_gu
            sh["xn3_epoch"] = isc
            if limit < 47 or (440 <= limit <= 469):
                return
            t_pr_all = [(s_pw[0], s_pw[0].n), (s_pw[1], s_pw[1].n)]
            E("scalar", lambda e: e.copy(sumpr_s[:], sums_ps[:, 0:512]),
              waits=[t_spr_last, free.get("sumpr")], inc=s_act)
            t_sumpr = E("scalar", lambda e: e.copy(spc_a[0:1, 0:1], sumpr_s[0:1, 0:1]),
                        inc=s_act)
            free["sums"] = t_sumpr
            free["sumx3"] = t_gmm
            yield 1000

            if limit < 48 or (440 <= limit <= 469):
                return
            t_out_sc = []
            t_corr_last_all = None
            for th in range(2):
                t_pieces = []
                for dh in range(2):
                    pidx = th * 2 + dh
                    def emit_dn_loads(h_):
                        w_ = h_ % 4
                        tl = E("sync", (lambda hh, ww, dh_: lambda e: e.dma_start(
                            out=wddh[ww],
                            in_=wdbf[hh][:, dh_ * 1024:dh_ * 1024 + 1024]))(h_, w_, dh),
                            waits=[free.get(("wddh", w_)), scr_tok_d[h_]],
                            inc=s_rd[w_], inc_k=16)
                        p_ = h_ % 2
                        tp = E("sync", (lambda hh, pp, th_: lambda e: e.dma_start(
                            out=prb[pp][:], in_=prs[hh][:, th_ * 256:th_ * 256 + 256]))(h_, p_, th),
                            waits=[free.get(("prb", p_)), t_pr_all], inc=s_pb[p_], inc_k=16)
                        return tl, tp

                    ld_pre = emit_dn_loads(0)
                    for hc in range(NHC):
                        wsl = hc % 4
                        psl = hc % 2
                        t_ld, t_lp = ld_pre
                        if hc + 1 < NHC:
                            ld_pre = emit_dn_loads(hc + 1)
                        tlast = None
                        for dc8 in range(8):
                            wts = []
                            if hc == 0 and dc8 % 2 == 0:
                                wts += [dn_ev[dc8], dn_ev[dc8 + 1]]
                                if pidx == 0:
                                    wts.append((free.get(("gps", 0)), free.get(("gps", 1)),
                                                free.get(("ups", 0)), free.get(("ups", 1))
                                                )[dc8 // 2])
                            if dc8 == 0:
                                wts += [t_ld, t_lp]
                            tlast = E("tensor", (lambda h_, s_, p_, d8_: lambda e: e.matmul(
                                ps[:, d8_ * 256:d8_ * 256 + 256],
                                wddh[s_][:, d8_ * 128:d8_ * 128 + 128], prb[p_][:],
                                start=(h_ == 0 and d8_ % 2 == 0), stop=False,
                                skip_group_check=True))(hc, wsl, psl, dc8),
                                waits=wts, inc=(s_pe if dc8 == 7 else None))
                        free[("wddh", wsl)] = tlast
                        free[("prb", psl)] = tlast
                        yield 1400
                    t_corr = None
                    for dc8 in range(8):
                        t_corr = E("tensor", (lambda d8_, th_: lambda e: e.matmul(
                            ps[:, d8_ * 256:d8_ * 256 + 256], c128_s[:],
                            sumpr_s[0:1, th_ * 256:th_ * 256 + 256],
                            start=False, stop=(d8_ % 2 == 1),
                            skip_group_check=True))(dc8, th),
                            waits=([t_sumpr] if dc8 == 0 else []), inc=s_pe)
                    t_corr_last_all = t_corr
                    yield 400
                    for dc8 in range(8):
                        dc = dh * 8 + dc8
                        esl = dc % 2
                        E("scalar", (lambda dc_, d8_, s_: lambda e: e.activation(
                            dfm[s_][:], ps[:, d8_ * 256:d8_ * 256 + 256],
                            AF.Copy, scale=sd_s[:, dc_:dc_ + 1]))(dc, dc8, esl),
                            waits=[t_corr, free.get(("dfm", esl))], inc=s_act)
                        t_df = E("scalar", (lambda s_: lambda e: e.copy(
                            spc_a[:], dfm[s_][:, 0:1]))(esl), inc=s_act)
                        dn_ev[dc8] = t_df
                        for jj in range(2):
                            t_ev = emit_tr(
                                dfm[esl][:, jj * 128:jj * 128 + 128],
                                dtm[jj][:, dc * 128:dc * 128 + 128],
                                [t_df])
                            t_pieces.append(t_ev)
                        free[("dfm", esl)] = t_pieces[-1]
                        yield 1000
                t_dtm_done = t_pieces[-1]
                for jj in range(2):
                    tc = th * 2 + jj
                    g = 4 * isc + tc
                    t_rb = E("sync", (lambda g_: lambda e: e.dma_start(
                        out=pool4[2][:], in_=x2s[g_]))(g),
                        waits=[free.get(("pool4", 2)), x2w_tok[isc]], inc=s_rb, inc_k=16)
                    t_nm, t_rs = emit_ln(
                        (lambda j_: lambda g_: dtm[j_][:, g_ * 512:g_ * 512 + 512])(jj),
                        4, [t_dtm_done], fset=True)
                    t_t4 = E("vector", (lambda j_: lambda e: e.tensor_scalar(
                        pool4[3][:], dtm[j_][:], nmf_s[:, 0:1], rstdf_s[:, 0:1],
                        ALU.add, ALU.mult))(jj),
                        waits=[t_nm, t_rs, free.get(("pool4", 3))], inc=s_dve)
                    free["rstdf"] = t_t4
                    t_o = E("vector", lambda e: e.tensor_add(
                        pool4[3][:], pool4[3][:], pool4[2][:]),
                        waits=[t_t4, t_rb], inc=s_dve)
                    free[("pool4", 2)] = t_o
                    t_st = E("sync", (lambda g_: lambda e: e.dma_start(
                        out=out_rows[g_], in_=pool4[3][:]))(g),
                        waits=[t_o], inc=s_out, inc_k=16)
                    free[("pool4", 3)] = t_st
                    t_out_sc.append(t_st)
                    yield 4500
            free["prs_hbm"] = [(x, x.n) for x in s_pb]
            free["sumpr"] = t_corr_last_all

        def drive():
            gw = gen_wcast()
            gm0 = gen_mamba(0)
            cw = cm0 = 0.0
            wdone = m0done = False
            while not (wdone and m0done):
                if (not m0done) and (wdone or cm0 <= cw):
                    try:
                        y = next(gm0)
                        if not isinstance(y, tuple):
                            cm0 += y
                    except StopIteration:
                        m0done = True
                else:
                    try:
                        cw += next(gw)
                    except StopIteration:
                        wdone = True
            def mamba_chain():
                for i in range(1, n_sc):
                    yield from gen_mamba(i)

            def ffn_chain():
                for i in range(n_sc):
                    yield from gen_ffn(i)

            gm = mamba_chain()
            gf = ffn_chain()
            cf = cm = 0.0
            fdone = mdone = False
            mneed = fneed = None
            while not (fdone and mdone):
                if mneed is not None and sh["xn3_epoch"] >= mneed:
                    mneed = None
                    if _PEG:
                        cm = cf = max(cm, cf)
                if fneed is not None and sh["mamba_done"] >= fneed:
                    fneed = None
                    if _PEG:
                        cm = cf = max(cm, cf)
                can_m = (not mdone) and mneed is None
                can_f = (not fdone) and fneed is None
                if not (can_m or can_f):
                    raise RuntimeError("emission deadlock")
                pick_m = can_m and ((not can_f) or cm <= cf)
                if pick_m:
                    try:
                        y = next(gm)
                        if isinstance(y, tuple):
                            mneed = y[1]
                        else:
                            cm += y
                    except StopIteration:
                        mdone = True
                else:
                    try:
                        y = next(gf)
                        if isinstance(y, tuple):
                            fneed = y[1]
                        else:
                            cf += y
                    except StopIteration:
                        fdone = True
        drive()

        import os as _os
        for _ in range(int(_os.environ.get("EXTRA_NOPS", "0"))):
            E("sync", lambda e: e.nop())
        if _os.environ.get("DBG_DUMP") == "1":
            dbg_specs = [("d_ytm0", ytm[0], BF16), ("d_xn1fm", xn1fm, BF16),
                         ("d_carry", carry, F32), ("d_negA", negA, F32),
                         ("d_delta0", delta[0], F32), ("d_dtm0", dtm[0], BF16),
                         ("d_x2", pool4[1], F32), ("d_bc16", bc16, BF16),
                         ("d_dtr", dtr_s, BF16), ("d_h0", h_t[0], F32),
                         ("d_da0", da_t[0], F32), ("d_dbu0", dbu_t[0], F32),
                         ("d_u0", u_t[0], F32), ("d_brep0", brep_all, BF16),
                         ("d_yb0", yb, BF16), ("d_xn3", xn3fm, BF16)]
            wfin = [(s_dve, s_dve.n), (s_act, s_act.n), (s_pe, s_pe.n),
                    (s_pl, s_pl.n), (s_out, s_out.n)]
            for nmx, tile, dtx in dbg_specs:
                hh = nc.dram_tensor(nmx, list(tile.shape), dtx,
                                    kind="ExternalOutput").ap()
                E("sync", (lambda h_, t_: lambda e: e.dma_start(out=h_, in_=t_[:]))(hh, tile),
                  waits=wfin, inc=s_out, inc_k=16)
        dma_sems = [sems[k] for k in sems
                    if k not in ("s_pe", "s_dve", "s_act", "s_pl")]
        E("sync", lambda e: e.nop(),
          waits=[(sm, sm.n) for sm in dma_sems])

        blk = es.enter_context(nc.Block())

        @blk.sync
        def _(sync):
            ctx.replay("sync", sync)

        @blk.gpsimd
        def _(g):
            ctx.replay("gpsimd", g)

        @blk.vector
        def _(v):
            ctx.replay("vector", v)

        @blk.scalar
        def _(s):
            ctx.replay("scalar", s)

        @blk.tensor
        def _(t):
            ctx.replay("tensor", t)

    return nc


def _prep_host(inputs, n_sc=4):
    bf = ml_dtypes.bfloat16
    f32 = np.float32
    W_xproj = np.asarray(inputs["L0_attn_W_xproj"], f32)
    W_dt = np.asarray(inputs["L0_attn_W_dt"], f32)
    b_dt = np.asarray(inputs["L0_attn_b_dt"], f32)
    A_log = np.asarray(inputs["L0_attn_A_log"], f32)
    D_par = np.asarray(inputs["L0_attn_D_param"], f32)
    wqg = np.asarray(inputs["L0_ffn_wq_gate"], np.int32)
    wqu = np.asarray(inputs["L0_ffn_wq_up"], np.int32)
    wqd = np.asarray(inputs["L0_ffn_wq_down"], np.int32)
    sgv = np.asarray(inputs["L0_ffn_s_gate"], f32)
    suv = np.asarray(inputs["L0_ffn_s_up"], f32)
    sdv = np.asarray(inputs["L0_ffn_s_down"], f32)

    common = {
        "wxt": np.ascontiguousarray(W_xproj.T),
        "wdt": np.ascontiguousarray(W_dt.T),
        "alog": np.ascontiguousarray(
            A_log.reshape(NDC, P, NST).transpose(1, 0, 2).reshape(P, 128)),
        "bdt": np.ascontiguousarray(b_dt.reshape(NDC, P).T),
        "dpar": np.ascontiguousarray(D_par.reshape(NDC, P).T),
        "wg": np.ascontiguousarray(wqg.T.reshape(D, NHC, P).transpose(1, 0, 2)),
        "wu": np.ascontiguousarray(wqu.T.reshape(D, NHC, P).transpose(1, 0, 2)),
        "wd": np.ascontiguousarray(wqd.T.reshape(NHC, P, D)),
        "sg": np.ascontiguousarray(sgv.reshape(NHC, P).T),
        "su": np.ascontiguousarray(suv.reshape(NHC, P).T),
        "sd": np.ascontiguousarray(sdv.reshape(NDC, P).T),
        "sel": np.kron(np.eye(16, dtype=f32), np.ones((1, P), f32)).astype(bf),
        "idb": np.eye(P, dtype=f32).astype(bf),
    }
    x = np.asarray(inputs["L0_attn_x"], f32)
    tok = 512 * n_sc
    maps = []
    for b in range(x.shape[0]):
        m = dict(common)
        m["x"] = np.ascontiguousarray(x[b, :tok])
        maps.append(m)
    return maps


def kernel(**inputs):
    n_sc = 4
    nc = build_nc(n_sc)
    maps = _prep_host(inputs, n_sc)
    res = run_bass_kernel_spmd(nc, maps, core_ids=list(range(len(maps))))
    outs = [np.asarray(r["out"], np.float32) for r in res.results]
    return np.stack(outs, axis=0)
